# version 20
# baseline (speedup 1.0000x reference)
import numpy as np
import concourse.bass as bass
import concourse.mybir as mybir
from concourse.bass_utils import run_bass_kernel_spmd

F32 = mybir.dt.float32
BF16 = mybir.dt.bfloat16
AF = mybir.ActivationFunctionType
ALU = mybir.AluOpType

D = 1024
NH = 16
NC = 8
EPS_LN_OUT = 1e-5
EPS_LN_ATTN = 1e-3
EPS_QK = 1e-6


class Op:
    __slots__ = ("fn", "waits", "inc", "step", "selfwait")
    def __init__(self, fn, waits, inc, step, selfwait=None):
        self.fn, self.waits, self.inc, self.step = fn, waits, inc, step
        self.selfwait = selfwait


class Sched:
    ENGINES = ("tensor", "vector", "scalar", "gpsimd", "sync")

    QENG = {"qg": "gpsimd", "qs": "sync", "qc": "scalar"}

    def __init__(self, nc, sems):
        self.tlag = 2
        self.nc = nc
        self.sems = sems
        self.cnt = {k: 0 for k in sems}
        self.ops = {e: [] for e in self.ENGINES}

    def _upgrade(self, after):
        out = []
        for t in after:
            if not t:
                continue
            sname, v = t
            if not sname.startswith("e_") and sname != "cc":
                v = self.cnt[sname]
            out.append((sname, v))
        return tuple(out)

    def op(self, engine, fn, after=(), inc=True):
        incname = "e_" + engine
        prev = self.cnt[incname]
        self.cnt[incname] += 1
        tok = (incname, self.cnt[incname])
        lag = self.tlag if engine == "tensor" else 0
        self.ops[engine].append(Op(fn, self._upgrade(after), incname, 1,
                                   selfwait=(incname, max(prev - lag, 0))))
        return tok

    def dma(self, q, sem, out, in_, after=(), transpose=False):
        eng = self.QENG[q]
        waits = self._upgrade(after)
        self.cnt[sem] += 16
        tok = (sem, self.cnt[sem])
        if transpose:
            fn = lambda e, o=out, i=in_: e.dma_start(out=o, in_=i, transpose=True)
        else:
            fn = lambda e, o=out, i=in_: e.dma_start(out=o, in_=i)
        self.ops[eng].append(Op(fn, waits, sem, 16))
        return tok

    def collective(self, ins, outs, groups, after=()):
        self.cnt["cc"] += 1
        tok = ("cc", self.cnt["cc"])
        def fn(e, i=ins, o=outs, g=groups):
            return e.collective_compute("AllReduce", ALU.add, replica_groups=g,
                                        ins=i, outs=o)
        self.ops["gpsimd"].append(Op(fn, self._upgrade(after), "cc", 1))
        return tok

    def emit(self, block):
        binders = {"tensor": block.tensor, "vector": block.vector,
                   "scalar": block.scalar, "gpsimd": block.gpsimd,
                   "sync": block.sync}
        for ename in self.ENGINES:
            ops = self.ops[ename]
            sems = self.sems
            def body(eng, ops=ops, sems=sems, ename=ename):
                last = {}
                for o in ops:
                    if o.selfwait is not None:
                        last[o.selfwait[0]] = max(last.get(o.selfwait[0], 0),
                                                  o.selfwait[1])
                    for (s, v) in o.waits:
                        if last.get(s, 0) < v:
                            eng.wait_ge(sems[s], v)
                            last[s] = v
                    inst = o.fn(eng)
                    if o.selfwait is not None:
                        inst._wait_ge(sems[o.selfwait[0]], o.selfwait[1])
                    if o.inc is not None:
                        inst.then_inc(sems[o.inc], o.step)
            binders[ename](body)


def bc_last(ap2d, n):
    return ap2d.unsqueeze(-1).broadcast_to(tuple(list(ap2d.shape) + [n]))


def _ap(x):
    return x if isinstance(x, bass.AP) else x.ap()


def r8(x):
    return _ap(x).rearrange("p (c t) -> p c t", c=NC)


def build_nc(ntiles=16):
    nc = bass.Bass("TRN2", target_bir_lowering=False,
                   detect_race_conditions=False)
    nt = ntiles
    tok_n = nt * 128

    dx = nc.dram_tensor("x", [tok_n, D], F32, kind="ExternalInput")
    dy = nc.dram_tensor("y", [tok_n, D], F32, kind="ExternalOutput")
    dW = {w: nc.dram_tensor(w, [D, D], F32, kind="ExternalInput")
          for w in ("Wq", "Wk", "Wv", "Wq2", "Wk2", "Wv2", "Wo",
                    "WrT", "WkT", "WvT")}
    dwmod = nc.dram_tensor("wmod", [D, 6 * D], F32, kind="ExternalInput")
    drow = {r: nc.dram_tensor(r, [1, D], F32, kind="ExternalInput")
            for r in ("bq", "bk", "bv", "bq2", "bk2", "bv2", "bo", "gamma",
                      "beta", "mu_r", "mu_k", "crow")}
    doddm = nc.dram_tensor("oddm", [128, 8], F32, kind="ExternalInput")
    dhc = nc.dram_tensor("hcspill", [tok_n, D], BF16)
    dv2 = nc.dram_tensor("v2spill", [tok_n, D], BF16)
    dSin = nc.dram_tensor("ccSin", [128, 512], F32)
    dSout = nc.dram_tensor("ccSout", [128, 512], F32)
    dHin = nc.dram_tensor("ccHin", [128, 8], F32)
    dHout = nc.dram_tensor("ccHout", [128, 8], F32)

    xv = dx.ap().rearrange("(i p) n -> i p n", p=128)
    yv = dy.ap().rearrange("(i p) n -> i p n", p=128)
    hcv = dhc.ap().rearrange("(i p) n -> i p n", p=128)
    v2v = dv2.ap().rearrange("(i p) n -> i p n", p=128)
    wv = {k: v.ap().rearrange("(c p) n -> c p n", p=128) for k, v in dW.items()}
    wmv = dwmod.ap().rearrange("(c p) n -> c p n", p=128)
    groups = [[0, 1], [2, 3], [4, 5], [6, 7]]

    import contextlib
    stack = contextlib.ExitStack()
    with stack:
        sb = lambda name, shape, dt, **kw: stack.enter_context(
            nc.sbuf_tensor(name, shape, dt, **kw))
        pst = lambda name, shape: stack.enter_context(
            nc.psum_tensor(name, shape, F32))

        arena1 = sb("arena1", [128, 4 * NC * D], BF16)
        arena2 = sb("arena2", [128, 2 * NC * D], BF16)
        wmodb = sb("wmodb", [128, 2 * NC * 512], BF16,
                   is_transpose_dest=True)
        bca = sb("bca", [128, 5 * D], BF16)
        Sbf = sb("Sbf", [128, 1024], BF16)
        recip = sb("recip", [128, nt * 16], F32)
        stats = sb("stats", [128, 64], F32)
        ccst = sb("ccst", [128, 5], F32)
        xr_sl = [sb(f"xrT{i}", [128, D], BF16) for i in range(2)]
        xk_sl = [sb(f"xkT{i}", [128, D], BF16) for i in range(2)]
        x1_sl = [sb(f"x1b{i}", [128, D], BF16) for i in range(2)]
        cols = sb("cols", [128, 64], BF16)
        pcol = sb("pcol", [128, 16], BF16)
        oddm = sb("oddm_s", [128, 8], F32)
        evenm = sb("evenm", [128, 8], F32)
        halos = sb("halos", [128, 8], F32)
        rows = sb("rows", [1, 8 * D], BF16)
        rowb = sb("rowb", [1, 6 * D], BF16)
        hvar = sb("hvar", [1, D], F32)
        onesr = sb("onesr", [1, 128], BF16)
        one1 = sb("one1", [1, 1], BF16)
        xb = [sb(f"xb{i}", [128, D], BF16) for i in range(2)]
        hcb = [sb(f"hcb{i}", [128, D], BF16) for i in range(2)]
        scr32 = sb("scr32", [128, D], BF16)
        dumpV = sb("dumpV", [128, D], BF16)
        dumpS = sb("dumpS", [128, D], BF16)
        tp = [sb(f"tp{i}", [128, D], BF16) for i in range(7)]
        td = [sb(f"td{i}", [128, D], BF16, is_transpose_dest=True) for i in range(4)]
        tdh = sb("tdh", [128, D], BF16, is_transpose_dest=True)
        keepT = wmodb[:, 0:D]
        keepX = wmodb[:, D:2 * D]
        x2b = [wmodb[:, 2 * D:3 * D], wmodb[:, 3 * D:4 * D]]

        psA = pst("psA", [128, D])
        psB = pst("psB", [128, D])
        psC = pst("psC", [128, D])
        psS = pst("psS", [128, 512])
        psX = pst("psX", [128, 512])

        semnames = (["e_tensor", "e_vector", "e_scalar", "e_gpsimd", "e_sync",
                     "cc", "qx0", "qx1", "qw0", "qw1", "qw2", "qw3", "qa0",
                     "qa1", "qwm0", "qwm1", "qt0", "qt1", "qt2", "qt3", "qtk",
                     "qhcs", "qhcr", "qst", "qccS", "qccH", "qr0", "qr1",
                     "qodd", "qth", "qv2", "qwt"])
        sems = {n: stack.enter_context(nc.semaphore(n)) for n in semnames}
        S = Sched(nc, sems)

        A1 = lambda j, c: arena1[:, (j * NC + c) * D:(j * NC + c + 1) * D]
        A2 = lambda j, c: arena2[:, (j * NC + c) * D:(j * NC + c + 1) * D]
        BCA = {k: bca[:, i * D:(i + 1) * D] for i, k in
               enumerate(("ca1", "cb1", "sabo", "g2", "b2"))}
        ROW = {k: rows[:, i * D:(i + 1) * D] for i, k in enumerate(
            ("ca1", "cb1", "sa", "ga", "sb", "gb", "t5", "t6"))}
        ROWB = {k: rowb[:, i * D:(i + 1) * D] for i, k in enumerate(
            ("bq2f", "bk2f", "bv2f", "sbr", "sbk", "sabo"))}
        HROW = {}
        COL = {k: cols[:, i * 8:(i + 1) * 8] for i, k in enumerate(
            ("siluc", "sac", "murc", "mukc", "msbc", "sbc", "scr"))}

        te = lambda fn, after=(), inc=False: S.op("tensor", fn, after, inc)
        ve = lambda fn, after=(), inc=False: S.op("vector", fn, after, inc)
        se = lambda fn, after=(), inc=False: S.op("scalar", fn, after, inc)

        psx_chain = [None]

        def proj(psum, lhsT_buf, wslot_fn, after, bias_row=None):
            t = None
            for c in range(NC):
                lhs = lhsT_buf[:, c * 128:(c + 1) * 128]
                w = wslot_fn(c)
                st = (c == 0)
                sp = (c == NC - 1) and bias_row is None
                aft = after if c == 0 else ()
                te(lambda e, o=psum[:, 0:512], l=lhs, r=w[:, 0:512], s=st, p=sp:
                   e.matmul(o, l, r, start=s, stop=p), after=aft)
                t = te(lambda e, o=psum[:, 512:1024], l=lhs, r=w[:, 512:1024],
                       s=st, p=sp: e.matmul(o, l, r, start=s, stop=p),
                       inc=(bias_row is None and c == NC - 1))
            if bias_row is not None:
                te(lambda e, o=psum[:, 0:512], r=bias_row[0:1, 0:512]:
                   e.matmul(o, onesr[0:1, :], r, start=False, stop=True))
                t = te(lambda e, o=psum[:, 512:1024], r=bias_row[0:1, 512:1024]:
                       e.matmul(o, onesr[0:1, :], r, start=False, stop=True),
                       inc=True)
            return t

        def transpose(q, sem, dst, src, after):
            return S.dma(q, sem, r8(dst), r8(src), after=after, transpose=True)

        def row_to_cols(dst_col8, src_row, after):
            psXb = psX.ap().bitcast(BF16)
            tlast = None
            for c in range(NC):
                aft = list(after) + [psx_chain[0]] if c == 0 else ()
                tlast = te(lambda e, o=psXb[:, 2 * c:2 * c + 1],
                           i=src_row[0:1, c * 128:(c + 1) * 128]:
                           e.transpose(o, i, one1[0:1, 0:1]),
                           after=aft, inc=(c == NC - 1))
            src = psXb[:, 0:16].rearrange("p (h two) -> p h two", two=2)[:, :, 0]
            tok = ve(lambda e: e.tensor_copy(dst_col8, src),
                     after=[tlast], inc=True)
            psx_chain[0] = tok
            return tok

        def bcast_row2(dst_tile, src_row_bf, after):
            tok = None
            for h in range(2):
                sl = slice(h * 512, (h + 1) * 512)
                aft = list(after) + [psx_chain[0]] if h == 0 else [tok]
                t = te(lambda e, s=sl: e.matmul(psX[:, 0:512], onesr[0:1, :],
                                                src_row_bf[0:1, s],
                                                start=True, stop=True),
                       after=aft, inc=True)
                tok = ve(lambda e, s=sl: e.tensor_copy(dst_tile[:, s],
                                                       psX[:, 0:512]),
                         after=[t], inc=True)
            psx_chain[0] = tok
            return tok

        def mm8(colsrc, wslot_fn, off, after):
            t = None
            for c in range(NC):
                t = te(lambda e, o=psX[0:1, 0:512], l=colsrc[:, c:c + 1],
                       r=wslot_fn(c)[:, off:off + 512], s=(c == 0),
                       p=(c == NC - 1): e.matmul(o, l, r, start=s, stop=p),
                       after=after if c == 0 else (), inc=(c == NC - 1))
            return t

        hv_slots = [hvar[0:1, 0:D]]
        hv_guard = [[]]
        hv_i = [0]
        tok_hrow = {}
        def hload(name):
            sidx = 0
            hv_i[0] += 1
            ap = hv_slots[sidx]
            tok = S.dma("qs", f"qr{sidx}", ap, drow[name].ap(),
                        after=list(hv_guard[sidx]))
            hv_guard[sidx] = []
            HROW[name] = ap
            tok_hrow[name] = tok
            return ap, tok, sidx
        def hused(sidx, tok):
            hv_guard[sidx].append(tok)
        tok_oddm = S.dma("qs", "qodd", oddm[:, :], doddm.ap())
        ve(lambda e: e.memset(onesr[0:1, :], 1.0))
        ve(lambda e: e.memset(one1[0:1, :], 1.0))
        ve(lambda e: e.memset(ccst[:, 0:1], EPS_LN_OUT))
        ve(lambda e: e.memset(ccst[:, 1:2], EPS_LN_ATTN))
        ve(lambda e: e.memset(ccst[:, 2:3], -1.0 / D))
        ve(lambda e: e.memset(ccst[:, 3:4], -0.5))
        ve(lambda e: e.memset(ccst[:, 4:5], -1.0))
        tok_ones = ve(lambda e: e.memset(COL["scr"], 0.0), inc=True)
        tok_evenm = ve(lambda e: e.tensor_scalar(evenm[:, :], oddm[:, :], -1.0, 1.0,
                                                 ALU.mult, ALU.add),
                       after=[tok_oddm], inc=True)
        crow_ap, tok_crow, crow_s = hload("crow")
        tok_silu = se(lambda e: e.activation(ROW["t5"][0:1, :], crow_ap,
                                             AF.Silu), after=[tok_crow],
                      inc=True)
        hused(crow_s, tok_silu)
        tok_siluc = row_to_cols(COL["siluc"], ROW["t5"], [tok_silu, tok_ones])

        tok_w = {}
        tok_wc = {}
        def load_w(name, arena_fn, slot, after=()):
            sem = (f"qw{slot}" if arena_fn is A1 else f"qa{slot}")
            toks = []
            for c in range(NC):
                toks.append(S.dma("qg", sem, arena_fn(slot, c), wv[name][c],
                                  after=after if c == 0 else ()))
            tok_w[name] = toks[-1]
            tok_wc[name] = toks
        load_w("Wv", A2, 0)
        load_w("Wv2", A2, 1)

        tok_x = {}
        for i in range(min(2, nt)):
            tok_x[i] = S.dma("qg", f"qx{i % 2}", xb[i][:, :], xv[i])

        prod_mmlast = {}
        prodtok_td = {None: None}
        prod_ev = {0: None, 1: None}
        for c in range(NC):
            tt_c = S.dma("qs", "qwt", r8(td[c % 2]), r8(A2(0, c)),
                         after=[tok_wc["Wv"][c], prod_mmlast.get(c - 2)],
                         transpose=True)
            for h in range(2):
                t = None
                for e_ in range(NC):
                    aft = ([tt_c, prod_ev[h], tok_w["Wv2"]]
                           if e_ == 0 else ())
                    t = te(lambda e, o=psC[:, h * 512:(h + 1) * 512],
                           l=td[c % 2][:, e_ * 128:(e_ + 1) * 128],
                           r=A2(1, e_)[:, h * 512:(h + 1) * 512],
                           s=(e_ == 0), p=(e_ == NC - 1):
                           e.matmul(o, l, r, start=s, stop=p),
                           after=aft, inc=(e_ == NC - 1))
                prod_mmlast[c] = t
                prodtok_td[c % 2] = t
                prod_ev[h] = ve(lambda e, d=A2(0, c)[:, h * 512:(h + 1) * 512],
                                p=psC[:, h * 512:(h + 1) * 512]:
                                e.tensor_copy(d, p), after=[t], inc=True)
        tok_Wvv_mm = prod_mmlast[NC - 1]
        tok_Wvv_ev = prod_ev[1]

        WM = lambda half, c: wmodb[:, (half * NC + c) * 512:(half * NC + c + 1) * 512]
        tok_wm = {}
        modev = {}
        blknames = ("sa", "ca1", "ga", "sb", "cb1", "gb")
        tok_mod = {}
        for sbi in range(12):
            half = sbi % 2
            for c in range(NC):
                tok_wm[(sbi, c)] = S.dma(
                    "qg", f"qwm{half}", WM(half, c),
                    wmv[c][:, sbi * 512:(sbi + 1) * 512],
                    after=[modev.get(sbi - 2)])
            pap = (psA if half == 0 else psB)[0:1, 0:512]
            t = None
            for c in range(NC):
                aft = [tok_siluc, tok_wm[(sbi, c)], modev.get(sbi - 2)] if c == 0 \
                    else [tok_wm[(sbi, c)]]
                t = te(lambda e, o=pap, l=COL["siluc"][:, c:c + 1],
                       r=WM(half, c), s=(c == 0), p=(c == NC - 1):
                       e.matmul(o, l, r, start=s, stop=p),
                       after=aft, inc=(c == NC - 1))
            blk = blknames[sbi // 2]
            off = (sbi % 2) * 512
            tv = ve(lambda e, d=ROW[blk][0:1, off:off + 512], p=pap:
                    e.tensor_copy(d, p), after=[t], inc=True)
            modev[sbi] = tv
            tok_mod[blk] = tv
        mod_psA_last = modev[10]
        mod_psB_last = modev[11]

        rowop = lambda fn, after: ve(fn, after=after, inc=True)
        tok_ca1 = rowop(lambda e: e.tensor_scalar(ROW["ca1"][0:1, :],
                                                  ROW["ca1"][0:1, :], 1.0, None,
                                                  ALU.add), [tok_mod["ca1"]])
        tok_cb1 = rowop(lambda e: e.tensor_scalar(ROW["cb1"][0:1, :],
                                                  ROW["cb1"][0:1, :], 1.0, None,
                                                  ALU.add), [tok_mod["cb1"]])

        def clamp_row(name, after):
            src = ROW[name]
            t1 = rowop(lambda e: e.tensor_scalar(ROW["t5"][0:1, :], src[0:1, :],
                                                 0.0, None, ALU.is_ge), after)
            t2 = rowop(lambda e: e.tensor_scalar(ROW["t5"][0:1, :],
                                                 ROW["t5"][0:1, :], 2.0, 1.0,
                                                 ALU.mult, ALU.subtract), [t1])
            t3a = se(lambda e: e.activation(ROW["t6"][0:1, :], src[0:1, :],
                                            AF.Abs), after=[t2], inc=True)
            t3 = rowop(lambda e: e.tensor_scalar(ROW["t6"][0:1, :],
                                                 ROW["t6"][0:1, :],
                                                 0.25, None, ALU.max), [t3a])
            return rowop(lambda e: e.tensor_mul(src[0:1, :], ROW["t5"][0:1, :],
                                                ROW["t6"][0:1, :]), [t3])
        tok_gac = clamp_row("ga", [tok_mod["ga"], tok_siluc])
        tok_gbc = clamp_row("gb", [tok_mod["gb"], tok_gac])

        def bcast_from_f32(dst_tile, build_fn, after):
            tb = rowop(build_fn, after)
            return bcast_row2(dst_tile, ROW["t5"], [tb])

        tok_bca_ca1 = bcast_from_f32(
            BCA["ca1"], lambda e: e.tensor_copy(ROW["t5"][0:1, :],
                                                ROW["ca1"][0:1, :]), [tok_ca1])
        tok_bca_cb1 = bcast_from_f32(
            BCA["cb1"], lambda e: e.tensor_copy(ROW["t5"][0:1, :],
                                                ROW["cb1"][0:1, :]), [tok_cb1])
        gam_ap, tok_gam, gam_s = hload("gamma")
        tok_bca_g2 = bcast_from_f32(
            BCA["g2"], lambda e: e.tensor_mul(ROW["t5"][0:1, :], ROW["ga"][0:1, :],
                                              gam_ap),
            [tok_gac, tok_gam])
        hused(gam_s, tok_bca_g2)
        bet_ap, tok_bet, bet_s = hload("beta")
        tok_bca_b2 = bcast_from_f32(
            BCA["b2"], lambda e: e.tensor_mul(ROW["t5"][0:1, :], ROW["ga"][0:1, :],
                                              bet_ap),
            [tok_gac, tok_bet])
        hused(bet_s, tok_bca_b2)
        bo_ap, tok_bo, bo_s = hload("bo")
        tok_sabo_row = rowop(lambda e: e.tensor_add(ROW["t5"][0:1, :],
                                                    ROW["sa"][0:1, :], bo_ap),
                             [tok_mod["sa"], tok_bo, tok_bca_b2])
        hused(bo_s, tok_sabo_row)
        tok_saborb = rowop(lambda e: e.tensor_copy(ROWB["sabo"],
                                                   ROW["t5"][0:1, :]),
                           [tok_sabo_row])
        tok_bca_sabo = bcast_row2(BCA["sabo"], ROWB["sabo"], [tok_saborb])

        tok_sac = row_to_cols(COL["sac"], ROW["sa"], [tok_mod["sa"]])
        mur_ap, tok_mur, mur_s = hload("mu_r")
        t_mr = rowop(lambda e: e.tensor_copy(ROW["t5"][0:1, :], mur_ap),
                     [tok_mur, tok_bca_sabo])
        hused(mur_s, t_mr)
        tok_murc = row_to_cols(COL["murc"], ROW["t5"], [t_mr])
        muk_ap, tok_muk, muk_s = hload("mu_k")
        t_mk = rowop(lambda e: e.tensor_copy(ROW["t5"][0:1, :], muk_ap),
                     [tok_muk, tok_murc])
        hused(muk_s, t_mk)
        tok_mukc = row_to_cols(COL["mukc"], ROW["t5"], [t_mk])
        t_ms = rowop(lambda e: e.tensor_scalar(ROW["t5"][0:1, :], ROW["sb"][0:1, :],
                                               -1.0, None, ALU.mult),
                     [tok_mod["sb"], tok_mukc])
        tok_msbc = row_to_cols(COL["msbc"], ROW["t5"], [t_ms])
        tok_sbc = row_to_cols(COL["sbc"], ROW["sb"], [tok_mod["sb"]])

        load_w("Wq", A1, 0)
        load_w("Wk", A1, 1)
        load_w("Wq2", A1, 2)
        load_w("Wk2", A1, 3)

        def fold_chain(wfn1, b1name, wfn2, b2name, dstb, after):
            b1_ap, tok_b1, b1_s = hload(b1name)
            tok = None
            for half in range(2):
                off = half * 512
                t = mm8(COL["sac"], wfn1, off,
                        (list(after) + [psx_chain[0]]) if half == 0 else [tok])
                tok = ve(lambda e, o=off: e.scalar_tensor_tensor(
                    ROW["t6"][0:1, o:o + 512], psX[0:1, 0:512], 1.0,
                    b1_ap[0:1, o:o + 512], ALU.mult, ALU.add),
                    after=[t, tok_b1], inc=True)
            hused(b1_s, tok)
            psx_chain[0] = tok
            b2_ap, tok_b2, b2_s = hload(b2name)
            tcols = row_to_cols(COL["scr"], ROW["t6"], [tok])
            for half in range(2):
                off = half * 512
                t = mm8(COL["scr"], wfn2, off,
                        [tcols, psx_chain[0]] if half == 0 else [tok])
                tok = ve(lambda e, o=off: e.scalar_tensor_tensor(
                    dstb[0:1, o:o + 512], psX[0:1, 0:512], 1.0,
                    b2_ap[0:1, o:o + 512], ALU.mult, ALU.add),
                    after=[t, tok_b2], inc=True)
            hused(b2_s, tok)
            psx_chain[0] = tok
            return tok

        tok_bq2f = fold_chain(lambda c: A1(0, c), "bq", lambda c: A1(2, c), "bq2",
                              ROWB["bq2f"], [tok_sac, tok_w["Wq"], tok_w["Wq2"]])
        tok_bk2f = fold_chain(lambda c: A1(1, c), "bk", lambda c: A1(3, c), "bk2",
                              ROWB["bk2f"], [tok_w["Wk"], tok_w["Wk2"]])

        bv_ap, tok_bv, bv_s = hload("bv")
        tcpv = rowop(lambda e: e.tensor_copy(ROW["t5"][0:1, :], bv_ap),
                     [tok_bv, tok_msbc])
        hused(bv_s, tcpv)
        tcolsv = row_to_cols(COL["scr"], ROW["t5"], [tcpv])
        bv2_ap, tok_bv2r, bv2_s = hload("bv2")
        tokf = None
        tok_f1v = None
        for half in range(2):
            off = half * 512
            t = mm8(COL["scr"], lambda c: A2(1, c), off,
                    [tcolsv, psx_chain[0], tok_w["Wv2"]] if half == 0
                    else [tokf])
            tok_f1v = t
            tokf = ve(lambda e, o=off: e.scalar_tensor_tensor(
                ROW["t6"][0:1, o:o + 512], psX[0:1, 0:512], 1.0,
                bv2_ap[0:1, o:o + 512], ALU.mult, ALU.add),
                after=[t, tok_bv2r], inc=True)
        hused(bv2_s, tokf)
        psx_chain[0] = tokf
        for half in range(2):
            off = half * 512
            t = mm8(COL["sac"], lambda c: A2(0, c), off,
                    [tokf, tok_sac, tok_Wvv_ev] if half == 0 else [tokf])
            tokf = ve(lambda e, o=off: e.scalar_tensor_tensor(
                ROWB["bv2f"][0:1, o:o + 512], psX[0:1, 0:512], 1.0,
                ROW["t6"][0:1, o:o + 512], ALU.mult, ALU.add),
                after=[t], inc=True)
        psx_chain[0] = tokf
        tok_bv2f = tokf
        load_w("WvT", A2, 1, after=[tok_f1v, tok_Wvv_mm])

        tok_hc = [None] * nt
        tok_spill = [None] * nt
        tok_hcT = [None] * nt
        tok_recip = [None] * nt
        tok_xread = [None] * nt
        tok_sq = [None] * nt
        tok_S_pertile = [None] * nt
        tok_qe = [None] * nt
        tok_ke = [None] * nt
        tok_tqs = [None] * nt
        tok_tks = [None] * nt
        tok_tq2s = [None] * nt
        tok_tk2s = [None] * nt
        tok_tqT = [None] * nt
        tok_tkT = [None] * nt
        tok_tq2 = [None] * nt
        tok_tk2 = [None] * nt
        tok_tek = [None] * nt
        g_psA = [mod_psA_last]
        g_psB = [mod_psB_last]
        g_psC = [tok_Wvv_ev]
        g_td0 = [prodtok_td[0]]
        t_v2p = [None] * nt
        t_v2ev = [None] * nt
        t_v2st = [None] * nt

        def p1_A(i):
            st = stats[:, (i % 2) * 16:(i % 2) * 16 + 16]
            xsum, xsq, xm, w_, sd, rstd = (st[:, k:k + 1] for k in range(6))
            if i not in tok_x:
                tok_x[i] = S.dma("qg", f"qx{i % 2}", xb[i % 2][:, :], xv[i],
                                 after=[tok_xread[i - 2], tok_sq[i - 2]])
            xt = xb[i % 2]
            tva = ve(lambda e, x=xt, a=xsum: e.tensor_scalar(
                dumpV[:, :], x[:, :], 1.0, None, ALU.mult, op1=ALU.add,
                accum_out=a), after=[tok_x[i]], inc=True)
            tsa = se(lambda e, x=xt, a=xsq: e.activation(
                dumpS[:, :], x[:, :], AF.Square, accum_out=a),
                after=[tok_x[i], tok_xread[i - 2] if i >= 2 else None],
                inc=True)
            tok_sq[i] = tsa
            ve(lambda e, a=xsum, m=xm: e.tensor_scalar(m, a, 1.0 / D, None,
                                                       ALU.mult), after=[tva])
            tw = ve(lambda e, a=xsum, m=xm, q=xsq, w=w_: e.scalar_tensor_tensor(
                w, a, m, q, ALU.mult, ALU.subtract), after=[tsa], inc=True)
            tln = se(lambda e, w=w_, s=sd: e.activation(
                s, w, AF.Ln, bias=ccst[:, 0:1], scale=ccst[:, 2:3]),
                after=[tw, tok_ones], inc=True)
            trx = se(lambda e, s=sd, r=rstd: e.activation(
                r, s, AF.Exp, scale=ccst[:, 3:4]), after=[tln], inc=True)
            tok_xread[i] = tt_ = ve(lambda e, x=xt, m=xm, r=rstd:
                                    e.tensor_scalar(dumpV[:, :], x[:, :], m, r,
                                                    ALU.subtract, ALU.mult),
                                    after=[trx], inc=True)
            tok_hc[i] = ve(lambda e, o=hcb[i % 2]: e.tensor_mul(
                o[:, :], dumpV[:, :], BCA["ca1"][:, :]),
                after=[tt_, tok_bca_ca1,
                       tok_spill[i - 2] if i >= 2 else None,
                       tok_hcT[i - 2] if i >= 2 else None], inc=True)
            tok_spill[i] = S.dma("qs", "qhcs", hcv[i], hcb[i % 2][:, :],
                                 after=[tok_hc[i]])
            tok_hcT[i] = transpose("qs", "qt0", td[0], hcb[i % 2],
                                   after=[tok_hc[i], g_td0[0]])

        def p1_B(i):
            tq = proj(psA, td[0], lambda c: A1(0, c),
                      after=[tok_hcT[i], tok_w["Wq"], g_psA[0]])
            tk = proj(psB, td[0], lambda c: A1(1, c),
                      after=[tok_w["Wk"], g_psB[0]])
            tv2 = proj(psC, td[0], lambda c: A2(0, c),
                       after=[tok_bv2f, g_psC[0]], bias_row=ROWB["bv2f"])
            t_v2p[i] = tv2
            g_td0[0] = tv2
            tok_tqs[i] = ve(lambda e: e.tensor_copy(tp[0][:, :], psA[:, :]),
                            after=[tq], inc=True)
            tok_tks[i] = se(lambda e: e.copy(tp[1][:, :], psB[:, :]),
                            after=[tk, tok_ke[i - 1] if i >= 1 else None],
                            inc=True)
            g_psB[0] = tok_tks[i]
            t_v2ev[i] = se(lambda e: e.copy(tp[6][:, :], psC[:, :]),
                           after=[tv2, t_v2st[i - 1] if i >= 1 else None],
                           inc=True)
            g_psC[0] = t_v2ev[i]
            t_v2st[i] = S.dma("qs", "qv2", v2v[i], tp[6][:, :],
                              after=[t_v2ev[i]])
            tok_tqT[i] = transpose("qs", "qt1", td[1], tp[0],
                                   after=[tok_tqs[i], tok_tq2[i - 1]
                                          if i >= 1 else prodtok_td[1]])
            tok_tkT[i] = transpose("qc", "qt2", td[2], tp[1],
                                   after=[tok_tks[i], tok_tk2[i - 1]
                                          if i >= 1 else None])

        def p1_C(i):
            st = stats[:, (i % 2) * 16:(i % 2) * 16 + 16]
            qksl = stats[:, 32 + (i % 2) * 16:32 + (i % 2) * 16 + 16]
            tq2 = proj(psC, td[1], lambda c: A1(2, c),
                       after=[tok_tqT[i], tok_w["Wq2"], tok_bq2f, g_psC[0]],
                       bias_row=ROWB["bq2f"])
            tok_tq2[i] = tq2
            tk2 = proj(psA, td[2], lambda c: A1(3, c),
                       after=[tok_tkT[i], tok_w["Wk2"], tok_bk2f, tok_tqs[i]],
                       bias_row=ROWB["bk2f"])
            tok_tk2[i] = tk2
            tq2s = ve(lambda e: e.tensor_copy(tp[0][:, :], psC[:, :]),
                      after=[tq2, tok_tqT[i]], inc=True)
            tok_tq2s[i] = tq2s
            g_psC[0] = tq2s
            tmq = ve(lambda e: e.tensor_single_scalar(scr32[:, :], tp[0][:, :],
                                                      0.0, ALU.min),
                     after=[tq2s, tok_tek[i - 1] if i >= 1 else None], inc=True)
            teq = se(lambda e: e.activation(tp[4][:, :], scr32[:, :], AF.Exp),
                     after=[tmq, tok_ke[i - 1] if i >= 1 else None], inc=True)
            tqe = ve(lambda e: e.scalar_tensor_tensor(
                tp[2][:, :], tp[0][:, :], 0.0, tp[4][:, :], ALU.max, ALU.add),
                after=[teq, tok_S_pertile[i - 1] if i >= 1 else None], inc=True)
            tok_qe[i] = tqe
            tk2s = ve(lambda e: e.tensor_copy(tp[1][:, :], psA[:, :]),
                      after=[tk2, tok_tkT[i]], inc=True)
            tok_tk2s[i] = tk2s
            g_psA[0] = tk2s
            tmk = ve(lambda e: e.tensor_single_scalar(scr32[:, :], tp[1][:, :],
                                                      0.0, ALU.min),
                     after=[tk2s, teq], inc=True)
            tek = se(lambda e: e.activation(tp[4][:, :], scr32[:, :], AF.Exp),
                     after=[tmk, tqe], inc=True)
            tok_tek[i] = tek
            tke = ve(lambda e: e.scalar_tensor_tensor(
                tp[3][:, :], tp[1][:, :], 0.0, tp[4][:, :], ALU.max, ALU.add),
                after=[tek, tok_S_pertile[i - 1] if i >= 1 else None], inc=True)
            tok_ke[i] = tke
            tpq = ve(lambda e: e.tensor_mul(tp[5][:, :], tp[2][:, :],
                                            tp[3][:, :]),
                     after=[tke], inc=True)
            trd = ve(lambda e, q=qksl: e.reduce_sum(
                q.unsqueeze(-1), tp[5].ap().rearrange("p (h w) -> p h w", h=NH),
                axis=mybir.AxisListType.X), after=[tpq], inc=True)
            tad = ve(lambda e, q=qksl: e.tensor_scalar(q, q, EPS_QK, None,
                                                       ALU.add),
                     after=[trd], inc=True)
            tok_recip[i] = ve(lambda e, q=qksl,
                              o=recip[:, i * 16:(i + 1) * 16]:
                              e.reciprocal(o, q), after=[tad], inc=True)

        def p1_D(i):
            tS = None
            for h in range(NH):
                half = h % 2
                pr = h // 2
                dstps = (psS if half == 0 else psX)
                tS = te(lambda e, o=dstps[half * 64:half * 64 + 64,
                                          pr * 64:(pr + 1) * 64],
                        l=tp[3][:, h * 64:(h + 1) * 64],
                        r=tp[2][:, h * 64:(h + 1) * 64],
                        s=(i == 0 and h < 2), p=(i == nt - 1 and h >= NH - 2),
                        tpz=(0, half * 64):
                        e.matmul(o, l, r, start=s, stop=p, tile_position=tpz),
                        after=([tok_qe[i], tok_ke[i]]
                               + ([psx_chain[0]] if i == 0 else []))
                        if h <= 1 else (), inc=(h == NH - 1))
            tok_S_pertile[i] = tS

        for k in range(nt + 2):
            if 0 <= k - 1 < nt:
                p1_B(k - 1)
            if 0 <= k - 2 < nt:
                p1_D(k - 2)
            if k < nt:
                p1_A(k)
            if 0 <= k - 1 < nt:
                p1_C(k - 1)
        tok_S_last = tok_S_pertile[nt - 1]

        tok_Sev1 = ve(lambda e: e.tensor_copy(scr32[0:64, 0:512],
                                              psS[0:64, :]),
                      after=[tok_S_last], inc=True)
        tok_Sevac = ve(lambda e: e.tensor_copy(scr32[64:128, 0:512],
                                               psX[64:128, :]),
                       after=[tok_Sev1], inc=True)
        psx_chain[0] = tok_Sevac
        tok_Sdma = S.dma("qg", "qccS", dSin.ap(), scr32[:, 0:512],
                         after=[tok_Sevac])
        tok_cc1 = S.collective([dSin.ap().opt()], [dSout.ap().opt()], groups,
                               after=[tok_Sdma])
        tok_Sld = S.dma("qg", "qccS", scr32[:, 0:512], dSout.ap(),
                        after=[tok_cc1])
        tzs = ve(lambda e: e.memset(Sbf[:, :], 0.0),
                 after=[tok_S_pertile[nt - 1]], inc=True)
        sb3 = Sbf.ap().rearrange("p (q c) -> p q c", c=128)
        tpv = scr32.ap().rearrange("p (q c) -> p q c", c=64)
        tev = ve(lambda e: e.tensor_copy(sb3[0:64, :, 0:64],
                                         tpv[0:64, 0:8, :]),
                 after=[tzs, tok_Sld], inc=True)
        tok_Sbf = ve(lambda e: e.tensor_copy(sb3[64:128, :, 64:128],
                                             tpv[64:128, 0:8, :]),
                     after=[tev], inc=True)

        load_w("Wo", A1, 0, after=[tok_S_last])
        load_w("WrT", A1, 1, after=[tok_S_last])
        load_w("WkT", A1, 2, after=[tok_S_last])

        def bias_row_from(slot, dstb, after):
            tok = None
            for half in range(2):
                off = half * 512
                t = mm8(COL["sbc"], lambda c: A1(slot, c), off,
                        (list(after) + [psx_chain[0]]) if half == 0 else [tok])
                tok = ve(lambda e, o=off: e.tensor_copy(dstb[0:1, o:o + 512],
                                                        psX[0:1, 0:512]),
                         after=[t], inc=True)
            psx_chain[0] = tok
            return tok
        tok_sbr = tok_sbk = tok_gbt = tok_wvt = None

        def emit_ffn_setup():
            nonlocal tok_sbr, tok_sbk, tok_gbt, tok_wvt
            tok_sbr = bias_row_from(1, ROWB["sbr"], [tok_sbc, tok_w["WrT"]])
            tok_sbk = bias_row_from(2, ROWB["sbk"], [tok_w["WkT"]])
            tok_gbt = bcast_row2(dumpV, ROW["gb"], [tok_gbc, tok_hc[nt - 1]])
            for c in range(NC):
                tok_wvt = ve(lambda e, cc=c: e.tensor_mul(A2(1, cc), A2(1, cc),
                                                          dumpV[:, :]),
                             after=[tok_gbt, tok_w["WvT"]] if c == 0 else (),
                             inc=(c == NC - 1))

        fence = tok_S_last
        order = [nt - 1] + list(range(nt - 1))
        NP = nt
        tok_store = []
        tok_pc = [None, None]
        Z = lambda: [None] * (NP + 6)
        t_hcr, t_x2, t_v2l, t_v2T = (Z(), Z(), Z(), Z())
        t_out, t_os, t_oT, t_ao, t_y, t_x1a, t_x1, t_s1, t_h2c, t_h2cT = (
            Z(), Z(), Z(), Z(), Z(), Z(), Z(), Z(), Z(), Z())
        t_mixr, t_mixk, t_r, t_rs, t_kf, t_kr, t_ka, t_kaT, t_vf, t_t3 = (
            Z(), Z(), Z(), Z(), Z(), Z(), Z(), Z(), Z(), Z())
        t_rsx = Z()
        t_vc = Z()
        t_st = Z()

        murb = bc_last(COL["murc"], 128)
        mukb = bc_last(COL["mukc"], 128)

        def p2_P(p):
            j = order[p]
            if p < 2:
                xg = [tok_xread[nt - 2 + p] if nt >= 2 else None,
                      tok_sq[nt - 2 + p] if nt >= 2 else None]
                hg = [tok_spill[nt - 2 + p] if nt >= 2 else None,
                      tok_hcT[nt - 2 + p] if nt >= 2 else None]
            else:
                xg = [t_x1a[p - 2]]
                hg = [t_y[p - 2]]
            t_x2[p] = S.dma("qg", f"qx{p % 2}", xb[p % 2][:, :], xv[j],
                            after=xg)
            t_hcr[p] = S.dma("qs", "qhcr", hcb[p % 2][:, :], hcv[j],
                             after=[tok_spill[nt - 1]] + hg)

        def p2_pref(p):
            j = order[p]
            t_v2l[p] = S.dma("qs", "qv2", tp[0][:, :], v2v[j],
                             after=[t_v2T[p - 1] if p >= 1
                                    else tok_qe[nt - 1]])
            t_v2T[p] = transpose("qc", "qt2", td[2], tp[0],
                                 after=[t_v2l[p], t_out[p - 1] if p >= 1
                                        else tok_tk2[nt - 1]])

        def p2_out(p):
            j = order[p]
            to_ = None
            for pr in range(NH // 2):
                to_ = te(lambda e, o=psA[:, pr * 128:(pr + 1) * 128],
                         l=td[2][:, pr * 128:(pr + 1) * 128],
                         r=Sbf[:, pr * 128:(pr + 1) * 128],
                         s=(pr == 0 or pr == 4), pp=(pr == 3 or pr == 7):
                         e.matmul(o, l, r, start=s, stop=pp),
                         after=([t_v2T[p], tok_Sbf,
                                 t_os[p - 1] if p >= 1 else tok_tk2s[nt - 1]]
                                if pr == 0 else ()),
                         inc=(pr == NH // 2 - 1))
            t_out[p] = to_
            rexp = bc_last(recip[:, j * 16:(j + 1) * 16], 64)
            t_os[p] = ve(lambda e, rx=rexp: e.tensor_tensor(
                tp[2].ap().rearrange("p (h w) -> p h w", h=NH),
                psA.ap().rearrange("p (h w) -> p h w", h=NH), rx, ALU.mult),
                after=[to_, tok_recip[j],
                       t_oT[p - 1] if p >= 1 else fence], inc=True)
            t_oT[p] = transpose("qc", "qt3", td[3], tp[2],
                                after=[t_os[p],
                                       t_ao[p - 1] if p >= 1 else fence])

        def p2_ao(p):
            t_ao[p] = proj(psC, td[3], lambda c: A1(0, c),
                           after=[t_oT[p], tok_w["Wo"], tok_saborb,
                                  t_y[p - 1] if p >= 1
                                  else tok_tq2s[nt - 1]],
                           bias_row=ROWB["sabo"])

        def p2_R(p):
            j = order[p]
            st1 = stats[:, 0:16] if p % 2 == 0 else stats[:, 16:32]
            (ysum, ysq, ym, yw, ysd, yrstd, x1sum, x1sq, x1m, x1w, x1sd,
             x1rstd) = (st1[:, k:k + 1] for k in range(12))
            scg = t_s1[p - 2] if p >= 2 else None
            ty = ve(lambda e, hc_=hcb[p % 2]: e.scalar_tensor_tensor(
                tp[6][:, :], psC[:, :], 1.0, hc_[:, :], ALU.bypass, ALU.add),
                after=[t_ao[p], t_hcr[p],
                       t_t3[p - 2] if p >= 2 else t_v2st[nt - 1]], inc=True)
            t_y[p] = ty
            tva = ve(lambda e, a=ysum: e.tensor_scalar(
                dumpV[:, :], tp[6][:, :], 1.0, None, ALU.mult, op1=ALU.add,
                accum_out=a), after=[ty], inc=True)
            tsa = se(lambda e, a=ysq: e.activation(
                dumpS[:, :], tp[6][:, :], AF.Square, accum_out=a),
                after=[ty, scg], inc=True)
            ve(lambda e, a=ysum, m=ym: e.tensor_scalar(m, a, 1.0 / D, None,
                                                       ALU.mult), after=[tva])
            tw1 = ve(lambda e, a=ysum, m=ym, q=ysq, w=yw:
                     e.scalar_tensor_tensor(w, a, m, q, ALU.mult,
                                            ALU.subtract),
                     after=[tsa], inc=True)
            tln1 = se(lambda e, w=yw, s=ysd: e.activation(
                s, w, AF.Ln, bias=ccst[:, 1:2], scale=ccst[:, 2:3]),
                after=[tw1], inc=True)
            trx1 = se(lambda e, s=ysd, r=yrstd: e.activation(
                r, s, AF.Exp, scale=ccst[:, 3:4]), after=[tln1], inc=True)
            tt1 = ve(lambda e, m=ym, r=yrstd: e.tensor_scalar(
                tp[4][:, :], tp[6][:, :], m, r, ALU.subtract, ALU.mult),
                after=[trx1], inc=True)
            tu = ve(lambda e: e.tensor_mul(tp[5][:, :], tp[4][:, :],
                                           BCA["g2"][:, :]),
                    after=[tt1, tok_bca_g2], inc=True)
            tx1a = ve(lambda e, xb_=xb[p % 2]: e.tensor_add(
                dumpV[:, :], tp[5][:, :], xb_[:, :]),
                after=[tu, t_x2[p]], inc=True)
            t_x1a[p] = tx1a
            x1_buf = keepX if j == nt - 1 else x1_sl[p % 2]
            tx1 = ve(lambda e, o=x1_buf: e.tensor_add(o[:, :], dumpV[:, :],
                                                      BCA["b2"][:, :]),
                     after=[tx1a, tok_bca_b2,
                            t_t3[p - 2] if p >= 2 else None], inc=True)
            t_x1[p] = tx1
            tvb = ve(lambda e, a=x1sum, o=x1_buf: e.tensor_scalar(
                dumpV[:, :], o[:, :], 1.0, None, ALU.mult, op1=ALU.add,
                accum_out=a), after=[tx1], inc=True)
            tsb2 = se(lambda e, a=x1sq, o=x1_buf: e.activation(
                dumpS[:, :], o[:, :], AF.Square, accum_out=a),
                after=[tx1], inc=True)
            ve(lambda e, a=x1sum, m=x1m: e.tensor_scalar(m, a, 1.0 / D, None,
                                                         ALU.mult),
               after=[tvb])
            tw2 = ve(lambda e, a=x1sum, m=x1m, q=x1sq, w=x1w:
                     e.scalar_tensor_tensor(w, a, m, q, ALU.mult,
                                            ALU.subtract),
                     after=[tsb2], inc=True)
            tln2 = se(lambda e, w=x1w, s=x1sd: e.activation(
                s, w, AF.Ln, bias=ccst[:, 0:1], scale=ccst[:, 2:3]),
                after=[tw2], inc=True)
            trx2 = se(lambda e, s=x1sd, r=x1rstd: e.activation(
                r, s, AF.Exp, scale=ccst[:, 3:4]), after=[tln2], inc=True)
            ts1 = ve(lambda e, m=x1m, r=x1rstd, o=x1_buf: e.tensor_scalar(
                tp[4][:, :], o[:, :], m, r, ALU.subtract, ALU.mult),
                after=[trx2], inc=True)
            t_s1[p] = ts1
            th2c = ve(lambda e: e.tensor_mul(tp[5][:, :], tp[4][:, :],
                                             BCA["cb1"][:, :]),
                      after=[ts1, tok_bca_cb1], inc=True)
            h2cT_buf = keepT if j == nt - 1 else tdh
            th2cT = transpose("qs", "qtk" if j == nt - 1 else "qth",
                              h2cT_buf, tp[5],
                              after=[th2c] + ([modev[11]] if j == nt - 1
                                              else [t_mixk[p - 1]]))
            t_h2cT[p] = th2cT
            if j == nt - 1:
                tvh = ve(lambda e: e.tensor_tensor(
                    halos.ap().unsqueeze(-1), r8(keepT)[:, :, 127:128],
                    evenm.ap().unsqueeze(-1), ALU.mult),
                    after=[th2cT, tok_evenm], inc=True)
                thd = S.dma("qg", "qccH", dHin.ap(), halos[:, :], after=[tvh])
                tcc2 = S.collective([dHin.ap().opt()], [dHout.ap().opt()],
                                    groups, after=[thd])
                thr = S.dma("qg", "qccH", halos[:, :], dHout.ap(),
                            after=[tcc2])
                tb1 = ve(lambda e: e.tensor_mul(scr32[:, 0:8], halos[:, :],
                                                oddm[:, :]),
                         after=[thr, tok_oddm], inc=True)
                tb2 = ve(lambda e: e.tensor_mul(scr32[:, 8:16], COL["msbc"],
                                                evenm[:, :]),
                         after=[tb1, tok_msbc, tok_evenm], inc=True)
                tok_pc[0] = ve(lambda e: e.tensor_add(pcol[:, 0:8],
                                                      scr32[:, 0:8],
                                                      scr32[:, 8:16]),
                               after=[tb2], inc=True)
                t_mixk[p] = th2c
                return
            pcr = pcol[:, (j % 2) * 8:(j % 2) * 8 + 8]
            t1a = ve(lambda e: e.tensor_copy(
                r8(tp[1])[:, :, 1:128], r8(tdh)[:, :, 0:127]),
                after=[th2cT, tok_pc[j % 2],
                       t_kr[p - 2] if p >= 2 else None], inc=True)
            t1b = ve(lambda e, pc_=pcr: e.tensor_copy(
                r8(tp[1])[:, :, 0:1], pc_.unsqueeze(-1)), after=[t1a],
                inc=True)
            if j != nt - 1:
                pcw = pcol[:, ((j + 1) % 2) * 8:((j + 1) % 2) * 8 + 8]
                tok_pc[(j + 1) % 2] = ve(lambda e, pw=pcw: e.tensor_copy(
                    pw.unsqueeze(-1), r8(tdh)[:, :, 127:128]),
                    after=[th2cT], inc=True)
            td_ = ve(lambda e: e.tensor_sub(tp[5][:, :], tdh[:, :],
                                            tp[1][:, :]),
                     after=[t1b, th2cT], inc=True)
            tm1 = ve(lambda e: e.tensor_tensor(
                r8(dumpV), r8(tp[5]), murb, ALU.mult),
                after=[td_, tok_murc], inc=True)
            txr = ve(lambda e, o=xr_sl[p % 2]: e.tensor_add(
                o[:, :], dumpV[:, :], tp[1][:, :]),
                after=[tm1, t_r[p - 2] if p >= 2 else None], inc=True)
            t_mixr[p] = txr
            tm2 = ve(lambda e: e.tensor_tensor(
                r8(dumpV), r8(tp[5]), mukb, ALU.mult),
                after=[txr, tok_mukc], inc=True)
            txk = ve(lambda e, o=xk_sl[p % 2]: e.tensor_add(
                o[:, :], dumpV[:, :], tp[1][:, :]),
                after=[tm2, t_kf[p - 2] if p >= 2 else None], inc=True)
            t_mixk[p] = txk

        def p2_F_r(pf):
            t_r[pf] = proj(psB, xr_sl[pf % 2], lambda c: A1(1, c),
                           after=[t_mixr[pf], tok_w["WrT"], tok_sbr,
                                  t_kr[pf]],
                           bias_row=ROWB["sbr"])
            t_rsx[pf] = se(lambda e: e.activation(dumpS[:, :], psB[:, :],
                                                  AF.Exp, scale=ccst[:, 4:5]),
                           after=[t_r[pf]], inc=True)
            ta1 = ve(lambda e: e.tensor_scalar(dumpS[:, :], dumpS[:, :], 1.0,
                                               None, ALU.add),
                     after=[t_rsx[pf]], inc=True)
            def _recip(e, nc=nc):
                with nc.allow_low_precision(reason="sigmoid tail, bf16 act"):
                    return e.reciprocal(dumpS[:, :], dumpS[:, :])
            t_rs[pf] = ve(_recip, after=[ta1], inc=True)

        def p2_F_kproj(pf, extra=()):
            t_kf[pf] = proj(psB, xk_sl[pf % 2], lambda c: A1(2, c),
                            after=[t_mixk[pf], tok_w["WkT"], tok_sbk,
                                   t_vc[pf - 1] if t_vc[pf - 1] is not None
                                   else tok_tks[nt - 1]] + list(extra),
                            bias_row=ROWB["sbk"])
            t_kr[pf] = ve(lambda e: e.tensor_single_scalar(
                tp[1][:, :], psB[:, :], 0.0, ALU.max),
                after=[t_kf[pf], t_mixk[pf + 1]], inc=True)
            t_ka[pf] = ve(lambda e: e.tensor_mul(tp[3][:, :], tp[1][:, :],
                                                 tp[1][:, :]),
                          after=[t_kr[pf], t_kaT[pf - 1]], inc=True)

        def p2_F_kaT(pf):
            t_kaT[pf] = transpose("qc", "qt2", td[0], tp[3],
                                  after=[t_ka[pf],
                                         t_vf[pf - 1] if t_vf[pf - 1]
                                         is not None else t_v2p[nt - 1]])

        def p2_F_v(pf, j, x1_buf):
            t_vf[pf] = proj(psB, td[0], lambda c: A2(1, c),
                            after=[t_kaT[pf], tok_wvt, t_rsx[pf]])
            t_vc[pf] = se(lambda e: e.copy(tp[3][:, :], psB[:, :]),
                          after=[t_vf[pf]], inc=True)
            t_t3[pf] = ve(lambda e: e.scalar_tensor_tensor(
                dumpV[:, :], tp[3][:, :], 1.0, dumpS[:, :], ALU.bypass,
                ALU.mult), after=[t_vc[pf], t_rs[pf]], inc=True)
            tx2 = ve(lambda e, o=x2b[pf % 2], xo=x1_buf: e.tensor_add(
                o[:, :], dumpV[:, :], xo[:, :]),
                after=[t_t3[pf], t_x1[pf], t_st[pf - 2] if pf >= 2 else None,
                       modev[11]],
                inc=True)
            t_st[pf] = S.dma("qg", "qst", yv[j], x2b[pf % 2][:, :],
                             after=[tx2])
            tok_store.append(t_st[pf])

        p2_P(0)
        p2_pref(0)
        for p in range(NP + 2):
            jq = order[p] if p < NP else None
            pf = p - 2
            jf = order[pf] if 0 <= pf < NP else None
            do_f = jf is not None and jf != nt - 1
            if jq is not None:
                p2_out(p)
            if p + 1 < NP:
                p2_pref(p + 1)
                p2_P(p + 1)
            if do_f:
                p2_F_kproj(pf)
                p2_F_r(pf)
            if jq is not None:
                p2_ao(p)
            if do_f:
                p2_F_kaT(pf)
                p2_F_v(pf, jf, x1_sl[pf % 2])
            if jq is not None:
                p2_R(p)
            if p == 1:
                emit_ffn_setup()

        jlast = nt - 1
        plast = NP + 2
        pcr = pcol[:, (jlast % 2) * 8:(jlast % 2) * 8 + 8]
        lastg = [t_t3[NP - 1], t_mixk[NP - 1], t_kr[NP - 1], t_rs[NP - 1]]
        t1a = ve(lambda e: e.tensor_copy(
            r8(tp[1])[:, :, 1:128], r8(keepT)[:, :, 0:127]),
            after=[t_h2cT[0], tok_pc[jlast % 2]] + lastg, inc=True)
        t1b = ve(lambda e: e.tensor_copy(
            r8(tp[1])[:, :, 0:1], pcr.unsqueeze(-1)), after=[t1a], inc=True)
        td_ = ve(lambda e: e.tensor_sub(tp[5][:, :], keepT[:, :], tp[1][:, :]),
                 after=[t1b], inc=True)
        tm1 = ve(lambda e: e.tensor_tensor(
            r8(dumpV), r8(tp[5]), murb, ALU.mult), after=[td_, tok_murc],
            inc=True)
        txr = ve(lambda e: e.tensor_add(xr_sl[0][:, :], dumpV[:, :],
                                        tp[1][:, :]),
                 after=[tm1, t_r[NP - 2]], inc=True)
        t_mixr[plast] = txr
        tm2 = ve(lambda e: e.tensor_tensor(
            r8(dumpV), r8(tp[5]), mukb, ALU.mult), after=[txr, tok_mukc],
            inc=True)
        txk = ve(lambda e: e.tensor_add(xk_sl[0][:, :], dumpV[:, :],
                                        tp[1][:, :]),
                 after=[tm2, t_kf[NP - 2]], inc=True)
        t_mixk[plast] = txk
        t_y[plast + 1] = txk
        t_s1[plast + 1] = txk
        t_mixk[plast + 1] = txk
        t_vf[plast - 1] = t_vf[NP - 1]
        p2_F_r(plast)
        p2_F_kproj(plast, extra=[t_rs[plast]])
        p2_F_kaT(plast)
        p2_F_v(plast, jlast, keepX)

        finals = {}
        for (s, v) in tok_store:
            finals[s] = max(finals.get(s, 0), v)
        for s, v in finals.items():
            S.ops["gpsimd"].append(Op(lambda e, ss=s, vv=v:
                                      e.wait_ge(sems[ss], vv), (), None, 1))

        with nc.Block() as block:
            S.emit(block)

    return nc


_NC_CACHE = {}

def _get_nc(ntiles=16):
    if ntiles not in _NC_CACHE:
        _NC_CACHE[ntiles] = build_nc(ntiles)
    return _NC_CACHE[ntiles]


def kernel(**inputs):
    L = int(inputs["L0_attn_x"].shape[1])
    toks = L // 2
    ntiles = toks // 128
    nc = _get_nc(ntiles)
    x = np.ascontiguousarray(np.asarray(inputs["L0_attn_x"], np.float32))
    row = lambda k: np.ascontiguousarray(np.asarray(inputs[k],
                                                    np.float32)).reshape(1, D)
    base = {
        "Wq": np.ascontiguousarray(np.asarray(inputs["L0_attn_Wq"], np.float32)),
        "Wk": np.ascontiguousarray(np.asarray(inputs["L0_attn_Wk"], np.float32)),
        "Wv": np.ascontiguousarray(np.asarray(inputs["L0_attn_Wv"], np.float32)),
        "Wq2": np.ascontiguousarray(np.asarray(inputs["L0_attn_Wq2"],
                                               np.float32)),
        "Wk2": np.ascontiguousarray(np.asarray(inputs["L0_attn_Wk2"],
                                               np.float32)),
        "Wv2": np.ascontiguousarray(np.asarray(inputs["L0_attn_Wv2"],
                                               np.float32)),
        "Wo": np.ascontiguousarray(np.asarray(inputs["L0_attn_Wo"], np.float32)),
        "WrT": np.ascontiguousarray(np.asarray(inputs["L0_ffn_W_r"],
                                               np.float32).T),
        "WkT": np.ascontiguousarray(np.asarray(inputs["L0_ffn_W_k"],
                                               np.float32).T),
        "WvT": np.ascontiguousarray(np.asarray(inputs["L0_ffn_W_v"],
                                               np.float32).T),
        "wmod": np.ascontiguousarray(np.asarray(inputs["L0_wmod"], np.float32)),
        "bq": row("L0_attn_bq"), "bk": row("L0_attn_bk"), "bv": row("L0_attn_bv"),
        "bq2": row("L0_attn_bq2"), "bk2": row("L0_attn_bk2"),
        "bv2": row("L0_attn_bv2"), "bo": row("L0_attn_bo"),
        "gamma": row("L0_attn_gamma"), "beta": row("L0_attn_beta"),
        "mu_r": row("L0_ffn_mu_r"), "mu_k": row("L0_ffn_mu_k"),
    }
    cmat = np.ascontiguousarray(np.asarray(inputs["L0_c"], np.float32))
    in_maps = []
    for core in range(8):
        b = core // 2
        half = core % 2
        m = dict(base)
        m["x"] = np.ascontiguousarray(x[b, half * toks:(half + 1) * toks, :])
        m["crow"] = cmat[b].reshape(1, D).copy()
        m["oddm"] = np.full((128, 8), float(half), np.float32)
        in_maps.append(m)
    res = run_bass_kernel_spmd(nc, in_maps, core_ids=list(range(8)))
    out = np.empty((4, L, D), np.float32)
    for core in range(8):
        b = core // 2
        half = core % 2
        out[b, half * toks:(half + 1) * toks, :] = res.results[core]["y"]
    return out


if __name__ == "__main__":
    ncb = build_nc(2)
    print("built", ncb.n_instructions(), "instructions")


# version 21
# speedup vs baseline: 1.0005x; 1.0005x over previous
import numpy as np
import concourse.bass as bass
import concourse.mybir as mybir
from concourse.bass_utils import run_bass_kernel_spmd

F32 = mybir.dt.float32
BF16 = mybir.dt.bfloat16
AF = mybir.ActivationFunctionType
ALU = mybir.AluOpType

D = 1024
NH = 16
NC = 8
EPS_LN_OUT = 1e-5
EPS_LN_ATTN = 1e-3
EPS_QK = 1e-6


class Op:
    __slots__ = ("fn", "waits", "inc", "step", "selfwait")
    def __init__(self, fn, waits, inc, step, selfwait=None):
        self.fn, self.waits, self.inc, self.step = fn, waits, inc, step
        self.selfwait = selfwait


class Sched:
    ENGINES = ("tensor", "vector", "scalar", "gpsimd", "sync")

    QENG = {"qg": "gpsimd", "qs": "sync", "qc": "scalar"}

    def __init__(self, nc, sems):
        self.tlag = 2
        self.nc = nc
        self.sems = sems
        self.cnt = {k: 0 for k in sems}
        self.ops = {e: [] for e in self.ENGINES}

    def _upgrade(self, after):
        out = []
        for t in after:
            if not t:
                continue
            sname, v = t
            if not sname.startswith("e_") and sname != "cc":
                v = self.cnt[sname]
            out.append((sname, v))
        return tuple(out)

    def op(self, engine, fn, after=(), inc=True):
        incname = "e_" + engine
        prev = self.cnt[incname]
        self.cnt[incname] += 1
        tok = (incname, self.cnt[incname])
        lag = self.tlag if engine == "tensor" else 0
        self.ops[engine].append(Op(fn, self._upgrade(after), incname, 1,
                                   selfwait=(incname, max(prev - lag, 0))))
        return tok

    def dma(self, q, sem, out, in_, after=(), transpose=False):
        eng = self.QENG[q]
        waits = self._upgrade(after)
        self.cnt[sem] += 16
        tok = (sem, self.cnt[sem])
        if transpose:
            fn = lambda e, o=out, i=in_: e.dma_start(out=o, in_=i, transpose=True)
        else:
            fn = lambda e, o=out, i=in_: e.dma_start(out=o, in_=i)
        self.ops[eng].append(Op(fn, waits, sem, 16))
        return tok

    def collective(self, ins, outs, groups, after=()):
        self.cnt["cc"] += 1
        tok = ("cc", self.cnt["cc"])
        def fn(e, i=ins, o=outs, g=groups):
            return e.collective_compute("AllReduce", ALU.add, replica_groups=g,
                                        ins=i, outs=o)
        self.ops["gpsimd"].append(Op(fn, self._upgrade(after), "cc", 1))
        return tok

    def emit(self, block):
        binders = {"tensor": block.tensor, "vector": block.vector,
                   "scalar": block.scalar, "gpsimd": block.gpsimd,
                   "sync": block.sync}
        for ename in self.ENGINES:
            ops = self.ops[ename]
            sems = self.sems
            def body(eng, ops=ops, sems=sems, ename=ename):
                last = {}
                for o in ops:
                    if o.selfwait is not None:
                        last[o.selfwait[0]] = max(last.get(o.selfwait[0], 0),
                                                  o.selfwait[1])
                    for (s, v) in o.waits:
                        if last.get(s, 0) < v:
                            eng.wait_ge(sems[s], v)
                            last[s] = v
                    inst = o.fn(eng)
                    if o.selfwait is not None:
                        inst._wait_ge(sems[o.selfwait[0]], o.selfwait[1])
                    if o.inc is not None:
                        inst.then_inc(sems[o.inc], o.step)
            binders[ename](body)


def bc_last(ap2d, n):
    return ap2d.unsqueeze(-1).broadcast_to(tuple(list(ap2d.shape) + [n]))


def _ap(x):
    return x if isinstance(x, bass.AP) else x.ap()


def r8(x):
    return _ap(x).rearrange("p (c t) -> p c t", c=NC)


def build_nc(ntiles=16):
    nc = bass.Bass("TRN2", target_bir_lowering=False,
                   detect_race_conditions=False)
    nt = ntiles
    tok_n = nt * 128

    dx = nc.dram_tensor("x", [tok_n, D], F32, kind="ExternalInput")
    dy = nc.dram_tensor("y", [tok_n, D], F32, kind="ExternalOutput")
    dW = {w: nc.dram_tensor(w, [D, D], F32, kind="ExternalInput")
          for w in ("Wq", "Wk", "Wv", "Wq2", "Wk2", "Wv2", "Wo",
                    "WrT", "WkT", "WvT")}
    dwmod = nc.dram_tensor("wmod", [D, 6 * D], F32, kind="ExternalInput")
    drow = {r: nc.dram_tensor(r, [1, D], F32, kind="ExternalInput")
            for r in ("bq", "bk", "bv", "bq2", "bk2", "bv2", "bo", "gamma",
                      "beta", "mu_r", "mu_k", "crow")}
    doddm = nc.dram_tensor("oddm", [128, 8], F32, kind="ExternalInput")
    dhc = nc.dram_tensor("hcspill", [tok_n, D], BF16)
    dv2 = nc.dram_tensor("v2spill", [tok_n, D], BF16)
    dSin = nc.dram_tensor("ccSin", [128, 512], F32)
    dSout = nc.dram_tensor("ccSout", [128, 512], F32)
    dHin = nc.dram_tensor("ccHin", [128, 8], F32)
    dHout = nc.dram_tensor("ccHout", [128, 8], F32)

    xv = dx.ap().rearrange("(i p) n -> i p n", p=128)
    yv = dy.ap().rearrange("(i p) n -> i p n", p=128)
    hcv = dhc.ap().rearrange("(i p) n -> i p n", p=128)
    v2v = dv2.ap().rearrange("(i p) n -> i p n", p=128)
    wv = {k: v.ap().rearrange("(c p) n -> c p n", p=128) for k, v in dW.items()}
    wmv = dwmod.ap().rearrange("(c p) n -> c p n", p=128)
    groups = [[0, 1], [2, 3], [4, 5], [6, 7]]

    import contextlib
    stack = contextlib.ExitStack()
    with stack:
        sb = lambda name, shape, dt, **kw: stack.enter_context(
            nc.sbuf_tensor(name, shape, dt, **kw))
        pst = lambda name, shape: stack.enter_context(
            nc.psum_tensor(name, shape, F32))

        arena1 = sb("arena1", [128, 4 * NC * D], BF16)
        arena2 = sb("arena2", [128, 2 * NC * D], BF16)
        wmodb = sb("wmodb", [128, 2 * NC * 512], BF16,
                   is_transpose_dest=True)
        bca = sb("bca", [128, 5 * D], BF16)
        Sbf = sb("Sbf", [128, 1024], BF16)
        recip = sb("recip", [128, nt * 16], F32)
        stats = sb("stats", [128, 64], F32)
        ccst = sb("ccst", [128, 5], F32)
        xr_sl = [sb(f"xrT{i}", [128, D], BF16) for i in range(2)]
        xk_sl = [sb(f"xkT{i}", [128, D], BF16) for i in range(2)]
        x1_sl = [sb(f"x1b{i}", [128, D], BF16) for i in range(2)]
        cols = sb("cols", [128, 64], BF16)
        pcol = sb("pcol", [128, 16], BF16)
        oddm = sb("oddm_s", [128, 8], F32)
        evenm = sb("evenm", [128, 8], F32)
        halos = sb("halos", [128, 8], F32)
        rows = sb("rows", [1, 8 * D], BF16)
        rowb = sb("rowb", [1, 6 * D], BF16)
        hvar = sb("hvar", [1, D], F32)
        onesr = sb("onesr", [1, 128], BF16)
        one1 = sb("one1", [1, 1], BF16)
        xb = [sb(f"xb{i}", [128, D], BF16) for i in range(2)]
        hcb = [sb(f"hcb{i}", [128, D], BF16) for i in range(2)]
        scr32 = sb("scr32", [128, D], BF16)
        dumpV = sb("dumpV", [128, D], BF16)
        dumpS = sb("dumpS", [128, D], BF16)
        tp = [sb(f"tp{i}", [128, D], BF16) for i in range(7)]
        td = [sb(f"td{i}", [128, D], BF16, is_transpose_dest=True) for i in range(4)]
        tdh = sb("tdh", [128, D], BF16, is_transpose_dest=True)
        keepT = wmodb[:, 0:D]
        keepX = wmodb[:, D:2 * D]
        x2b = [wmodb[:, 2 * D:3 * D], wmodb[:, 3 * D:4 * D]]

        psA = pst("psA", [128, D])
        psB = pst("psB", [128, D])
        psC = pst("psC", [128, D])
        psS = pst("psS", [128, 512])
        psX = pst("psX", [128, 512])

        semnames = (["e_tensor", "e_vector", "e_scalar", "e_gpsimd", "e_sync",
                     "cc", "qx0", "qx1", "qw0", "qw1", "qw2", "qw3", "qa0",
                     "qa1", "qwm0", "qwm1", "qt0", "qt1", "qt2", "qt3", "qtk",
                     "qhcs", "qhcr", "qst", "qccS", "qccH", "qr0", "qr1",
                     "qodd", "qth", "qv2", "qwt"])
        sems = {n: stack.enter_context(nc.semaphore(n)) for n in semnames}
        S = Sched(nc, sems)

        A1 = lambda j, c: arena1[:, (j * NC + c) * D:(j * NC + c + 1) * D]
        A2 = lambda j, c: arena2[:, (j * NC + c) * D:(j * NC + c + 1) * D]
        BCA = {k: bca[:, i * D:(i + 1) * D] for i, k in
               enumerate(("ca1", "cb1", "sabo", "g2", "b2"))}
        ROW = {k: rows[:, i * D:(i + 1) * D] for i, k in enumerate(
            ("ca1", "cb1", "sa", "ga", "sb", "gb", "t5", "t6"))}
        ROWB = {k: rowb[:, i * D:(i + 1) * D] for i, k in enumerate(
            ("bq2f", "bk2f", "bv2f", "sbr", "sbk", "sabo"))}
        HROW = {}
        COL = {k: cols[:, i * 8:(i + 1) * 8] for i, k in enumerate(
            ("siluc", "sac", "murc", "mukc", "msbc", "sbc", "scr"))}

        te = lambda fn, after=(), inc=False: S.op("tensor", fn, after, inc)
        ve = lambda fn, after=(), inc=False: S.op("vector", fn, after, inc)
        se = lambda fn, after=(), inc=False: S.op("scalar", fn, after, inc)

        psx_chain = [None]

        def proj(psum, lhsT_buf, wslot_fn, after, bias_row=None):
            t = None
            for c in range(NC):
                lhs = lhsT_buf[:, c * 128:(c + 1) * 128]
                w = wslot_fn(c)
                st = (c == 0)
                sp = (c == NC - 1) and bias_row is None
                aft = after if c == 0 else ()
                te(lambda e, o=psum[:, 0:512], l=lhs, r=w[:, 0:512], s=st, p=sp:
                   e.matmul(o, l, r, start=s, stop=p), after=aft)
                t = te(lambda e, o=psum[:, 512:1024], l=lhs, r=w[:, 512:1024],
                       s=st, p=sp: e.matmul(o, l, r, start=s, stop=p),
                       inc=(bias_row is None and c == NC - 1))
            if bias_row is not None:
                te(lambda e, o=psum[:, 0:512], r=bias_row[0:1, 0:512]:
                   e.matmul(o, onesr[0:1, :], r, start=False, stop=True))
                t = te(lambda e, o=psum[:, 512:1024], r=bias_row[0:1, 512:1024]:
                       e.matmul(o, onesr[0:1, :], r, start=False, stop=True),
                       inc=True)
            return t

        def transpose(q, sem, dst, src, after):
            return S.dma(q, sem, r8(dst), r8(src), after=after, transpose=True)

        def row_to_cols(dst_col8, src_row, after):
            psXb = psX.ap().bitcast(BF16)
            tlast = None
            for c in range(NC):
                aft = list(after) + [psx_chain[0]] if c == 0 else ()
                tlast = te(lambda e, o=psXb[:, 2 * c:2 * c + 1],
                           i=src_row[0:1, c * 128:(c + 1) * 128]:
                           e.transpose(o, i, one1[0:1, 0:1]),
                           after=aft, inc=(c == NC - 1))
            src = psXb[:, 0:16].rearrange("p (h two) -> p h two", two=2)[:, :, 0]
            tok = ve(lambda e: e.tensor_copy(dst_col8, src),
                     after=[tlast], inc=True)
            psx_chain[0] = tok
            return tok

        def bcast_row2(dst_tile, src_row_bf, after):
            tok = None
            for h in range(2):
                sl = slice(h * 512, (h + 1) * 512)
                aft = list(after) + [psx_chain[0]] if h == 0 else [tok]
                t = te(lambda e, s=sl: e.matmul(psX[:, 0:512], onesr[0:1, :],
                                                src_row_bf[0:1, s],
                                                start=True, stop=True),
                       after=aft, inc=True)
                tok = ve(lambda e, s=sl: e.tensor_copy(dst_tile[:, s],
                                                       psX[:, 0:512]),
                         after=[t], inc=True)
            psx_chain[0] = tok
            return tok

        def mm8(colsrc, wslot_fn, off, after):
            t = None
            for c in range(NC):
                t = te(lambda e, o=psX[0:1, 0:512], l=colsrc[:, c:c + 1],
                       r=wslot_fn(c)[:, off:off + 512], s=(c == 0),
                       p=(c == NC - 1): e.matmul(o, l, r, start=s, stop=p),
                       after=after if c == 0 else (), inc=(c == NC - 1))
            return t

        hv_slots = [hvar[0:1, 0:D]]
        hv_guard = [[]]
        hv_i = [0]
        tok_hrow = {}
        def hload(name):
            sidx = 0
            hv_i[0] += 1
            ap = hv_slots[sidx]
            tok = S.dma("qs", f"qr{sidx}", ap, drow[name].ap(),
                        after=list(hv_guard[sidx]))
            hv_guard[sidx] = []
            HROW[name] = ap
            tok_hrow[name] = tok
            return ap, tok, sidx
        def hused(sidx, tok):
            hv_guard[sidx].append(tok)
        tok_oddm = S.dma("qs", "qodd", oddm[:, :], doddm.ap())
        ve(lambda e: e.memset(onesr[0:1, :], 1.0))
        ve(lambda e: e.memset(one1[0:1, :], 1.0))
        ve(lambda e: e.memset(ccst[:, 0:1], EPS_LN_OUT))
        ve(lambda e: e.memset(ccst[:, 1:2], EPS_LN_ATTN))
        ve(lambda e: e.memset(ccst[:, 2:3], -1.0 / D))
        ve(lambda e: e.memset(ccst[:, 3:4], -0.5))
        ve(lambda e: e.memset(ccst[:, 4:5], -1.0))
        tok_ones = ve(lambda e: e.memset(COL["scr"], 0.0), inc=True)
        tok_evenm = ve(lambda e: e.tensor_scalar(evenm[:, :], oddm[:, :], -1.0, 1.0,
                                                 ALU.mult, ALU.add),
                       after=[tok_oddm], inc=True)
        crow_ap, tok_crow, crow_s = hload("crow")
        tok_silu = se(lambda e: e.activation(ROW["t5"][0:1, :], crow_ap,
                                             AF.Silu), after=[tok_crow],
                      inc=True)
        hused(crow_s, tok_silu)
        tok_siluc = row_to_cols(COL["siluc"], ROW["t5"], [tok_silu, tok_ones])

        tok_w = {}
        tok_wc = {}
        def load_w(name, arena_fn, slot, after=()):
            sem = (f"qw{slot}" if arena_fn is A1 else f"qa{slot}")
            toks = []
            for c in range(NC):
                toks.append(S.dma("qg", sem, arena_fn(slot, c), wv[name][c],
                                  after=after if c == 0 else ()))
            tok_w[name] = toks[-1]
            tok_wc[name] = toks
        load_w("Wv", A2, 0)
        load_w("Wv2", A2, 1)

        tok_x = {}
        for i in range(min(2, nt)):
            tok_x[i] = S.dma("qg", f"qx{i % 2}", xb[i][:, :], xv[i])

        prod_mmlast = {}
        prodtok_td = {None: None}
        prod_ev = {0: None, 1: None}
        for c in range(NC):
            tt_c = S.dma("qs", "qwt", r8(td[c % 2]), r8(A2(0, c)),
                         after=[tok_wc["Wv"][c], prod_mmlast.get(c - 2)],
                         transpose=True)
            for h in range(2):
                t = None
                for e_ in range(NC):
                    aft = ([tt_c, prod_ev[h], tok_w["Wv2"]]
                           if e_ == 0 else ())
                    t = te(lambda e, o=psC[:, h * 512:(h + 1) * 512],
                           l=td[c % 2][:, e_ * 128:(e_ + 1) * 128],
                           r=A2(1, e_)[:, h * 512:(h + 1) * 512],
                           s=(e_ == 0), p=(e_ == NC - 1):
                           e.matmul(o, l, r, start=s, stop=p),
                           after=aft, inc=(e_ == NC - 1))
                prod_mmlast[c] = t
                prodtok_td[c % 2] = t
                prod_ev[h] = ve(lambda e, d=A2(0, c)[:, h * 512:(h + 1) * 512],
                                p=psC[:, h * 512:(h + 1) * 512]:
                                e.tensor_copy(d, p), after=[t], inc=True)
        tok_Wvv_mm = prod_mmlast[NC - 1]
        tok_Wvv_ev = prod_ev[1]

        WM = lambda half, c: wmodb[:, (half * NC + c) * 512:(half * NC + c + 1) * 512]
        tok_wm = {}
        modev = {}
        blknames = ("sa", "ca1", "ga", "sb", "cb1", "gb")
        tok_mod = {}
        for sbi in range(12):
            half = sbi % 2
            for c in range(NC):
                tok_wm[(sbi, c)] = S.dma(
                    "qg", f"qwm{half}", WM(half, c),
                    wmv[c][:, sbi * 512:(sbi + 1) * 512],
                    after=[modev.get(sbi - 2)])
            pap = (psA if half == 0 else psB)[0:1, 0:512]
            t = None
            for c in range(NC):
                aft = [tok_siluc, tok_wm[(sbi, c)], modev.get(sbi - 2)] if c == 0 \
                    else [tok_wm[(sbi, c)]]
                t = te(lambda e, o=pap, l=COL["siluc"][:, c:c + 1],
                       r=WM(half, c), s=(c == 0), p=(c == NC - 1):
                       e.matmul(o, l, r, start=s, stop=p),
                       after=aft, inc=(c == NC - 1))
            blk = blknames[sbi // 2]
            off = (sbi % 2) * 512
            tv = ve(lambda e, d=ROW[blk][0:1, off:off + 512], p=pap:
                    e.tensor_copy(d, p), after=[t], inc=True)
            modev[sbi] = tv
            tok_mod[blk] = tv
        mod_psA_last = modev[10]
        mod_psB_last = modev[11]

        rowop = lambda fn, after: ve(fn, after=after, inc=True)
        tok_ca1 = rowop(lambda e: e.tensor_scalar(ROW["ca1"][0:1, :],
                                                  ROW["ca1"][0:1, :], 1.0, None,
                                                  ALU.add), [tok_mod["ca1"]])
        tok_cb1 = rowop(lambda e: e.tensor_scalar(ROW["cb1"][0:1, :],
                                                  ROW["cb1"][0:1, :], 1.0, None,
                                                  ALU.add), [tok_mod["cb1"]])

        def clamp_row(name, after):
            src = ROW[name]
            t1 = rowop(lambda e: e.tensor_scalar(ROW["t5"][0:1, :], src[0:1, :],
                                                 0.0, None, ALU.is_ge), after)
            t2 = rowop(lambda e: e.tensor_scalar(ROW["t5"][0:1, :],
                                                 ROW["t5"][0:1, :], 2.0, 1.0,
                                                 ALU.mult, ALU.subtract), [t1])
            t3a = se(lambda e: e.activation(ROW["t6"][0:1, :], src[0:1, :],
                                            AF.Abs), after=[t2], inc=True)
            t3 = rowop(lambda e: e.tensor_scalar(ROW["t6"][0:1, :],
                                                 ROW["t6"][0:1, :],
                                                 0.25, None, ALU.max), [t3a])
            return rowop(lambda e: e.tensor_mul(src[0:1, :], ROW["t5"][0:1, :],
                                                ROW["t6"][0:1, :]), [t3])
        tok_gac = clamp_row("ga", [tok_mod["ga"], tok_siluc])
        tok_gbc = clamp_row("gb", [tok_mod["gb"], tok_gac])

        def bcast_from_f32(dst_tile, build_fn, after):
            tb = rowop(build_fn, after)
            return bcast_row2(dst_tile, ROW["t5"], [tb])

        tok_bca_ca1 = bcast_from_f32(
            BCA["ca1"], lambda e: e.tensor_copy(ROW["t5"][0:1, :],
                                                ROW["ca1"][0:1, :]), [tok_ca1])
        tok_bca_cb1 = bcast_from_f32(
            BCA["cb1"], lambda e: e.tensor_copy(ROW["t5"][0:1, :],
                                                ROW["cb1"][0:1, :]), [tok_cb1])
        gam_ap, tok_gam, gam_s = hload("gamma")
        tok_bca_g2 = bcast_from_f32(
            BCA["g2"], lambda e: e.tensor_mul(ROW["t5"][0:1, :], ROW["ga"][0:1, :],
                                              gam_ap),
            [tok_gac, tok_gam])
        hused(gam_s, tok_bca_g2)
        bet_ap, tok_bet, bet_s = hload("beta")
        tok_bca_b2 = bcast_from_f32(
            BCA["b2"], lambda e: e.tensor_mul(ROW["t5"][0:1, :], ROW["ga"][0:1, :],
                                              bet_ap),
            [tok_gac, tok_bet])
        hused(bet_s, tok_bca_b2)
        bo_ap, tok_bo, bo_s = hload("bo")
        tok_sabo_row = rowop(lambda e: e.tensor_add(ROW["t5"][0:1, :],
                                                    ROW["sa"][0:1, :], bo_ap),
                             [tok_mod["sa"], tok_bo, tok_bca_b2])
        hused(bo_s, tok_sabo_row)
        tok_saborb = rowop(lambda e: e.tensor_copy(ROWB["sabo"],
                                                   ROW["t5"][0:1, :]),
                           [tok_sabo_row])
        tok_bca_sabo = bcast_row2(BCA["sabo"], ROWB["sabo"], [tok_saborb])

        tok_sac = row_to_cols(COL["sac"], ROW["sa"], [tok_mod["sa"]])
        mur_ap, tok_mur, mur_s = hload("mu_r")
        t_mr = rowop(lambda e: e.tensor_copy(ROW["t5"][0:1, :], mur_ap),
                     [tok_mur, tok_bca_sabo])
        hused(mur_s, t_mr)
        tok_murc = row_to_cols(COL["murc"], ROW["t5"], [t_mr])
        muk_ap, tok_muk, muk_s = hload("mu_k")
        t_mk = rowop(lambda e: e.tensor_copy(ROW["t5"][0:1, :], muk_ap),
                     [tok_muk, tok_murc])
        hused(muk_s, t_mk)
        tok_mukc = row_to_cols(COL["mukc"], ROW["t5"], [t_mk])
        t_ms = rowop(lambda e: e.tensor_scalar(ROW["t5"][0:1, :], ROW["sb"][0:1, :],
                                               -1.0, None, ALU.mult),
                     [tok_mod["sb"], tok_mukc])
        tok_msbc = row_to_cols(COL["msbc"], ROW["t5"], [t_ms])
        tok_sbc = row_to_cols(COL["sbc"], ROW["sb"], [tok_mod["sb"]])

        load_w("Wq", A1, 0)
        load_w("Wk", A1, 1)
        load_w("Wq2", A1, 2)
        load_w("Wk2", A1, 3)

        def fold_chain(wfn1, b1name, wfn2, b2name, dstb, after):
            b1_ap, tok_b1, b1_s = hload(b1name)
            tok = None
            for half in range(2):
                off = half * 512
                t = mm8(COL["sac"], wfn1, off,
                        (list(after) + [psx_chain[0]]) if half == 0 else [tok])
                tok = ve(lambda e, o=off: e.scalar_tensor_tensor(
                    ROW["t6"][0:1, o:o + 512], psX[0:1, 0:512], 1.0,
                    b1_ap[0:1, o:o + 512], ALU.mult, ALU.add),
                    after=[t, tok_b1], inc=True)
            hused(b1_s, tok)
            psx_chain[0] = tok
            b2_ap, tok_b2, b2_s = hload(b2name)
            tcols = row_to_cols(COL["scr"], ROW["t6"], [tok])
            for half in range(2):
                off = half * 512
                t = mm8(COL["scr"], wfn2, off,
                        [tcols, psx_chain[0]] if half == 0 else [tok])
                tok = ve(lambda e, o=off: e.scalar_tensor_tensor(
                    dstb[0:1, o:o + 512], psX[0:1, 0:512], 1.0,
                    b2_ap[0:1, o:o + 512], ALU.mult, ALU.add),
                    after=[t, tok_b2], inc=True)
            hused(b2_s, tok)
            psx_chain[0] = tok
            return tok

        tok_bq2f = fold_chain(lambda c: A1(0, c), "bq", lambda c: A1(2, c), "bq2",
                              ROWB["bq2f"], [tok_sac, tok_w["Wq"], tok_w["Wq2"]])
        tok_bk2f = fold_chain(lambda c: A1(1, c), "bk", lambda c: A1(3, c), "bk2",
                              ROWB["bk2f"], [tok_w["Wk"], tok_w["Wk2"]])

        bv_ap, tok_bv, bv_s = hload("bv")
        tcpv = rowop(lambda e: e.tensor_copy(ROW["t5"][0:1, :], bv_ap),
                     [tok_bv, tok_msbc])
        hused(bv_s, tcpv)
        tcolsv = row_to_cols(COL["scr"], ROW["t5"], [tcpv])
        bv2_ap, tok_bv2r, bv2_s = hload("bv2")
        tokf = None
        tok_f1v = None
        for half in range(2):
            off = half * 512
            t = mm8(COL["scr"], lambda c: A2(1, c), off,
                    [tcolsv, psx_chain[0], tok_w["Wv2"]] if half == 0
                    else [tokf])
            tok_f1v = t
            tokf = ve(lambda e, o=off: e.scalar_tensor_tensor(
                ROW["t6"][0:1, o:o + 512], psX[0:1, 0:512], 1.0,
                bv2_ap[0:1, o:o + 512], ALU.mult, ALU.add),
                after=[t, tok_bv2r], inc=True)
        hused(bv2_s, tokf)
        psx_chain[0] = tokf
        for half in range(2):
            off = half * 512
            t = mm8(COL["sac"], lambda c: A2(0, c), off,
                    [tokf, tok_sac, tok_Wvv_ev] if half == 0 else [tokf])
            tokf = ve(lambda e, o=off: e.scalar_tensor_tensor(
                ROWB["bv2f"][0:1, o:o + 512], psX[0:1, 0:512], 1.0,
                ROW["t6"][0:1, o:o + 512], ALU.mult, ALU.add),
                after=[t], inc=True)
        psx_chain[0] = tokf
        tok_bv2f = tokf
        load_w("WvT", A2, 1, after=[tok_f1v, tok_Wvv_mm])

        tok_hc = [None] * nt
        tok_spill = [None] * nt
        tok_hcT = [None] * nt
        tok_recip = [None] * nt
        tok_xread = [None] * nt
        tok_sq = [None] * nt
        tok_S_pertile = [None] * nt
        tok_qe = [None] * nt
        tok_ke = [None] * nt
        tok_tqs = [None] * nt
        tok_tks = [None] * nt
        tok_tq2s = [None] * nt
        tok_tk2s = [None] * nt
        tok_tqT = [None] * nt
        tok_tkT = [None] * nt
        tok_tq2 = [None] * nt
        tok_tk2 = [None] * nt
        tok_tek = [None] * nt
        g_psA = [mod_psA_last]
        g_psB = [mod_psB_last]
        g_psC = [tok_Wvv_ev]
        g_td0 = [prodtok_td[0]]
        t_v2p = [None] * nt
        t_v2ev = [None] * nt
        t_v2st = [None] * nt

        def p1_A(i):
            st = stats[:, (i % 2) * 16:(i % 2) * 16 + 16]
            xsum, xsq, xm, w_, sd, rstd = (st[:, k:k + 1] for k in range(6))
            if i not in tok_x:
                tok_x[i] = S.dma("qg", f"qx{i % 2}", xb[i % 2][:, :], xv[i],
                                 after=[tok_xread[i - 2], tok_sq[i - 2]])
            xt = xb[i % 2]
            tva = ve(lambda e, x=xt, a=xsum: e.tensor_scalar(
                dumpV[:, :], x[:, :], 1.0, None, ALU.mult, op1=ALU.add,
                accum_out=a), after=[tok_x[i]], inc=True)
            tsa = se(lambda e, x=xt, a=xsq: e.activation(
                dumpS[:, :], x[:, :], AF.Square, accum_out=a),
                after=[tok_x[i], tok_xread[i - 2] if i >= 2 else None],
                inc=True)
            tok_sq[i] = tsa
            ve(lambda e, a=xsum, m=xm: e.tensor_scalar(m, a, 1.0 / D, None,
                                                       ALU.mult), after=[tva])
            tw = ve(lambda e, a=xsum, m=xm, q=xsq, w=w_: e.scalar_tensor_tensor(
                w, a, m, q, ALU.mult, ALU.subtract), after=[tsa], inc=True)
            tln = se(lambda e, w=w_, s=sd: e.activation(
                s, w, AF.Ln, bias=ccst[:, 0:1], scale=ccst[:, 2:3]),
                after=[tw, tok_ones], inc=True)
            trx = se(lambda e, s=sd, r=rstd: e.activation(
                r, s, AF.Exp, scale=ccst[:, 3:4]), after=[tln], inc=True)
            tok_xread[i] = tt_ = ve(lambda e, x=xt, m=xm, r=rstd:
                                    e.tensor_scalar(dumpV[:, :], x[:, :], m, r,
                                                    ALU.subtract, ALU.mult),
                                    after=[trx], inc=True)
            tok_hc[i] = ve(lambda e, o=hcb[i % 2]: e.tensor_mul(
                o[:, :], dumpV[:, :], BCA["ca1"][:, :]),
                after=[tt_, tok_bca_ca1,
                       tok_spill[i - 2] if i >= 2 else None,
                       tok_hcT[i - 2] if i >= 2 else None], inc=True)
            tok_spill[i] = S.dma("qs", "qhcs", hcv[i], hcb[i % 2][:, :],
                                 after=[tok_hc[i]])
            tok_hcT[i] = transpose("qs", "qt0", td[0], hcb[i % 2],
                                   after=[tok_hc[i], g_td0[0]])

        def p1_B(i):
            tq = proj(psA, td[0], lambda c: A1(0, c),
                      after=[tok_hcT[i], tok_w["Wq"], g_psA[0]])
            tk = proj(psB, td[0], lambda c: A1(1, c),
                      after=[tok_w["Wk"], g_psB[0]])
            tv2 = proj(psC, td[0], lambda c: A2(0, c),
                       after=[tok_bv2f, g_psC[0]], bias_row=ROWB["bv2f"])
            t_v2p[i] = tv2
            g_td0[0] = tv2
            tok_tqs[i] = se(lambda e: e.copy(tp[0][:, :], psA[:, :]),
                            after=[tq, tok_qe[i - 1] if i >= 1 else None],
                            inc=True)
            tok_tks[i] = se(lambda e: e.copy(tp[1][:, :], psB[:, :]),
                            after=[tk, tok_ke[i - 1] if i >= 1 else None],
                            inc=True)
            g_psB[0] = tok_tks[i]
            t_v2ev[i] = se(lambda e: e.copy(tp[6][:, :], psC[:, :]),
                           after=[tv2, t_v2st[i - 1] if i >= 1 else None],
                           inc=True)
            g_psC[0] = t_v2ev[i]
            t_v2st[i] = S.dma("qs", "qv2", v2v[i], tp[6][:, :],
                              after=[t_v2ev[i]])
            tok_tqT[i] = transpose("qs", "qt1", td[1], tp[0],
                                   after=[tok_tqs[i], tok_tq2[i - 1]
                                          if i >= 1 else prodtok_td[1]])
            tok_tkT[i] = transpose("qc", "qt2", td[2], tp[1],
                                   after=[tok_tks[i], tok_tk2[i - 1]
                                          if i >= 1 else None])

        def p1_C(i):
            st = stats[:, (i % 2) * 16:(i % 2) * 16 + 16]
            qksl = stats[:, 32 + (i % 2) * 16:32 + (i % 2) * 16 + 16]
            tq2 = proj(psC, td[1], lambda c: A1(2, c),
                       after=[tok_tqT[i], tok_w["Wq2"], tok_bq2f, g_psC[0]],
                       bias_row=ROWB["bq2f"])
            tok_tq2[i] = tq2
            tk2 = proj(psA, td[2], lambda c: A1(3, c),
                       after=[tok_tkT[i], tok_w["Wk2"], tok_bk2f, tok_tqs[i]],
                       bias_row=ROWB["bk2f"])
            tok_tk2[i] = tk2
            tq2s = ve(lambda e: e.tensor_copy(tp[0][:, :], psC[:, :]),
                      after=[tq2, tok_tqT[i]], inc=True)
            tok_tq2s[i] = tq2s
            g_psC[0] = tq2s
            tmq = ve(lambda e: e.tensor_single_scalar(scr32[:, :], tp[0][:, :],
                                                      0.0, ALU.min),
                     after=[tq2s, tok_tek[i - 1] if i >= 1 else None], inc=True)
            teq = se(lambda e: e.activation(tp[4][:, :], scr32[:, :], AF.Exp),
                     after=[tmq, tok_ke[i - 1] if i >= 1 else None], inc=True)
            tqe = ve(lambda e: e.scalar_tensor_tensor(
                tp[2][:, :], tp[0][:, :], 0.0, tp[4][:, :], ALU.max, ALU.add),
                after=[teq, tok_S_pertile[i - 1] if i >= 1 else None], inc=True)
            tok_qe[i] = tqe
            tk2s = ve(lambda e: e.tensor_copy(tp[1][:, :], psA[:, :]),
                      after=[tk2, tok_tkT[i]], inc=True)
            tok_tk2s[i] = tk2s
            g_psA[0] = tk2s
            tmk = ve(lambda e: e.tensor_single_scalar(scr32[:, :], tp[1][:, :],
                                                      0.0, ALU.min),
                     after=[tk2s, teq], inc=True)
            tek = se(lambda e: e.activation(tp[4][:, :], scr32[:, :], AF.Exp),
                     after=[tmk, tqe], inc=True)
            tok_tek[i] = tek
            tke = ve(lambda e: e.scalar_tensor_tensor(
                tp[3][:, :], tp[1][:, :], 0.0, tp[4][:, :], ALU.max, ALU.add),
                after=[tek, tok_S_pertile[i - 1] if i >= 1 else None], inc=True)
            tok_ke[i] = tke
            tpq = ve(lambda e: e.tensor_mul(tp[5][:, :], tp[2][:, :],
                                            tp[3][:, :]),
                     after=[tke], inc=True)
            trd = ve(lambda e, q=qksl: e.reduce_sum(
                q.unsqueeze(-1), tp[5].ap().rearrange("p (h w) -> p h w", h=NH),
                axis=mybir.AxisListType.X), after=[tpq], inc=True)
            tad = ve(lambda e, q=qksl: e.tensor_scalar(q, q, EPS_QK, None,
                                                       ALU.add),
                     after=[trd], inc=True)
            tok_recip[i] = ve(lambda e, q=qksl,
                              o=recip[:, i * 16:(i + 1) * 16]:
                              e.reciprocal(o, q), after=[tad], inc=True)

        def p1_D(i):
            tS = None
            for h in range(NH):
                half = h % 2
                pr = h // 2
                dstps = (psS if half == 0 else psX)
                tS = te(lambda e, o=dstps[half * 64:half * 64 + 64,
                                          pr * 64:(pr + 1) * 64],
                        l=tp[3][:, h * 64:(h + 1) * 64],
                        r=tp[2][:, h * 64:(h + 1) * 64],
                        s=(i == 0 and h < 2), p=(i == nt - 1 and h >= NH - 2),
                        tpz=(0, half * 64):
                        e.matmul(o, l, r, start=s, stop=p, tile_position=tpz),
                        after=([tok_qe[i], tok_ke[i]]
                               + ([psx_chain[0]] if i == 0 else []))
                        if h <= 1 else (), inc=(h == NH - 1))
            tok_S_pertile[i] = tS

        for k in range(nt + 2):
            if 0 <= k - 1 < nt:
                p1_B(k - 1)
            if 0 <= k - 2 < nt:
                p1_D(k - 2)
            if k < nt:
                p1_A(k)
            if 0 <= k - 1 < nt:
                p1_C(k - 1)
        tok_S_last = tok_S_pertile[nt - 1]

        tok_Sev1 = ve(lambda e: e.tensor_copy(scr32[0:64, 0:512],
                                              psS[0:64, :]),
                      after=[tok_S_last], inc=True)
        tok_Sevac = ve(lambda e: e.tensor_copy(scr32[64:128, 0:512],
                                               psX[64:128, :]),
                       after=[tok_Sev1], inc=True)
        psx_chain[0] = tok_Sevac
        tok_Sdma = S.dma("qg", "qccS", dSin.ap(), scr32[:, 0:512],
                         after=[tok_Sevac])
        tok_cc1 = S.collective([dSin.ap().opt()], [dSout.ap().opt()], groups,
                               after=[tok_Sdma])
        tok_Sld = S.dma("qg", "qccS", scr32[:, 0:512], dSout.ap(),
                        after=[tok_cc1])
        tzs = ve(lambda e: e.memset(Sbf[:, :], 0.0),
                 after=[tok_S_pertile[nt - 1]], inc=True)
        sb3 = Sbf.ap().rearrange("p (q c) -> p q c", c=128)
        tpv = scr32.ap().rearrange("p (q c) -> p q c", c=64)
        tev = ve(lambda e: e.tensor_copy(sb3[0:64, :, 0:64],
                                         tpv[0:64, 0:8, :]),
                 after=[tzs, tok_Sld], inc=True)
        tok_Sbf = ve(lambda e: e.tensor_copy(sb3[64:128, :, 64:128],
                                             tpv[64:128, 0:8, :]),
                     after=[tev], inc=True)

        load_w("Wo", A1, 0, after=[tok_S_last])
        load_w("WrT", A1, 1, after=[tok_S_last])
        load_w("WkT", A1, 2, after=[tok_S_last])

        def bias_row_from(slot, dstb, after):
            tok = None
            for half in range(2):
                off = half * 512
                t = mm8(COL["sbc"], lambda c: A1(slot, c), off,
                        (list(after) + [psx_chain[0]]) if half == 0 else [tok])
                tok = ve(lambda e, o=off: e.tensor_copy(dstb[0:1, o:o + 512],
                                                        psX[0:1, 0:512]),
                         after=[t], inc=True)
            psx_chain[0] = tok
            return tok
        tok_sbr = tok_sbk = tok_gbt = tok_wvt = None

        def emit_ffn_setup():
            nonlocal tok_sbr, tok_sbk, tok_gbt, tok_wvt
            tok_sbr = bias_row_from(1, ROWB["sbr"], [tok_sbc, tok_w["WrT"]])
            tok_sbk = bias_row_from(2, ROWB["sbk"], [tok_w["WkT"]])
            tok_gbt = bcast_row2(dumpV, ROW["gb"], [tok_gbc, tok_hc[nt - 1]])
            for c in range(NC):
                tok_wvt = ve(lambda e, cc=c: e.tensor_mul(A2(1, cc), A2(1, cc),
                                                          dumpV[:, :]),
                             after=[tok_gbt, tok_w["WvT"]] if c == 0 else (),
                             inc=(c == NC - 1))

        fence = tok_S_last
        order = [nt - 1] + list(range(nt - 1))
        NP = nt
        tok_store = []
        tok_pc = [None, None]
        Z = lambda: [None] * (NP + 6)
        t_hcr, t_x2, t_v2l, t_v2T = (Z(), Z(), Z(), Z())
        t_out, t_os, t_oT, t_ao, t_y, t_x1a, t_x1, t_s1, t_h2c, t_h2cT = (
            Z(), Z(), Z(), Z(), Z(), Z(), Z(), Z(), Z(), Z())
        t_mixr, t_mixk, t_r, t_rs, t_kf, t_kr, t_ka, t_kaT, t_vf, t_t3 = (
            Z(), Z(), Z(), Z(), Z(), Z(), Z(), Z(), Z(), Z())
        t_rsx = Z()
        t_st = Z()

        murb = bc_last(COL["murc"], 128)
        mukb = bc_last(COL["mukc"], 128)

        def p2_P(p):
            j = order[p]
            if p < 2:
                xg = [tok_xread[nt - 2 + p] if nt >= 2 else None,
                      tok_sq[nt - 2 + p] if nt >= 2 else None]
                hg = [tok_spill[nt - 2 + p] if nt >= 2 else None,
                      tok_hcT[nt - 2 + p] if nt >= 2 else None]
            else:
                xg = [t_x1a[p - 2]]
                hg = [t_y[p - 2]]
            t_x2[p] = S.dma("qg", f"qx{p % 2}", xb[p % 2][:, :], xv[j],
                            after=xg)
            t_hcr[p] = S.dma("qs", "qhcr", hcb[p % 2][:, :], hcv[j],
                             after=[tok_spill[nt - 1]] + hg)

        def p2_pref(p):
            j = order[p]
            t_v2l[p] = S.dma("qs", "qv2", tp[0][:, :], v2v[j],
                             after=[t_v2T[p - 1] if p >= 1
                                    else tok_qe[nt - 1]])
            t_v2T[p] = transpose("qc", "qt2", td[2], tp[0],
                                 after=[t_v2l[p], t_out[p - 1] if p >= 1
                                        else tok_tk2[nt - 1]])

        def p2_out(p):
            j = order[p]
            to_ = None
            for pr in range(NH // 2):
                to_ = te(lambda e, o=psA[:, pr * 128:(pr + 1) * 128],
                         l=td[2][:, pr * 128:(pr + 1) * 128],
                         r=Sbf[:, pr * 128:(pr + 1) * 128],
                         s=(pr == 0 or pr == 4), pp=(pr == 3 or pr == 7):
                         e.matmul(o, l, r, start=s, stop=pp),
                         after=([t_v2T[p], tok_Sbf,
                                 t_os[p - 1] if p >= 1 else tok_tk2s[nt - 1]]
                                if pr == 0 else ()),
                         inc=(pr == NH // 2 - 1))
            t_out[p] = to_
            rexp = bc_last(recip[:, j * 16:(j + 1) * 16], 64)
            t_os[p] = ve(lambda e, rx=rexp: e.tensor_tensor(
                tp[2].ap().rearrange("p (h w) -> p h w", h=NH),
                psA.ap().rearrange("p (h w) -> p h w", h=NH), rx, ALU.mult),
                after=[to_, tok_recip[j],
                       t_oT[p - 1] if p >= 1 else fence], inc=True)
            t_oT[p] = transpose("qc", "qt3", td[3], tp[2],
                                after=[t_os[p],
                                       t_ao[p - 1] if p >= 1 else fence])

        def p2_ao(p):
            t_ao[p] = proj(psC, td[3], lambda c: A1(0, c),
                           after=[t_oT[p], tok_w["Wo"], tok_saborb,
                                  t_y[p - 1] if p >= 1
                                  else tok_tq2s[nt - 1]],
                           bias_row=ROWB["sabo"])

        def p2_R(p):
            j = order[p]
            st1 = stats[:, 0:16] if p % 2 == 0 else stats[:, 16:32]
            (ysum, ysq, ym, yw, ysd, yrstd, x1sum, x1sq, x1m, x1w, x1sd,
             x1rstd) = (st1[:, k:k + 1] for k in range(12))
            scg = t_s1[p - 2] if p >= 2 else None
            ty = ve(lambda e, hc_=hcb[p % 2]: e.scalar_tensor_tensor(
                tp[6][:, :], psC[:, :], 1.0, hc_[:, :], ALU.bypass, ALU.add),
                after=[t_ao[p], t_hcr[p],
                       t_t3[p - 2] if p >= 2 else t_v2st[nt - 1]], inc=True)
            t_y[p] = ty
            tva = ve(lambda e, a=ysum: e.tensor_scalar(
                dumpV[:, :], tp[6][:, :], 1.0, None, ALU.mult, op1=ALU.add,
                accum_out=a), after=[ty], inc=True)
            tsa = se(lambda e, a=ysq: e.activation(
                dumpS[:, :], tp[6][:, :], AF.Square, accum_out=a),
                after=[ty, scg], inc=True)
            ve(lambda e, a=ysum, m=ym: e.tensor_scalar(m, a, 1.0 / D, None,
                                                       ALU.mult), after=[tva])
            tw1 = ve(lambda e, a=ysum, m=ym, q=ysq, w=yw:
                     e.scalar_tensor_tensor(w, a, m, q, ALU.mult,
                                            ALU.subtract),
                     after=[tsa], inc=True)
            tln1 = se(lambda e, w=yw, s=ysd: e.activation(
                s, w, AF.Ln, bias=ccst[:, 1:2], scale=ccst[:, 2:3]),
                after=[tw1], inc=True)
            trx1 = se(lambda e, s=ysd, r=yrstd: e.activation(
                r, s, AF.Exp, scale=ccst[:, 3:4]), after=[tln1], inc=True)
            tt1 = ve(lambda e, m=ym, r=yrstd: e.tensor_scalar(
                tp[4][:, :], tp[6][:, :], m, r, ALU.subtract, ALU.mult),
                after=[trx1], inc=True)
            tu = ve(lambda e: e.tensor_mul(tp[5][:, :], tp[4][:, :],
                                           BCA["g2"][:, :]),
                    after=[tt1, tok_bca_g2], inc=True)
            tx1a = ve(lambda e, xb_=xb[p % 2]: e.tensor_add(
                dumpV[:, :], tp[5][:, :], xb_[:, :]),
                after=[tu, t_x2[p]], inc=True)
            t_x1a[p] = tx1a
            x1_buf = keepX if j == nt - 1 else x1_sl[p % 2]
            tx1 = ve(lambda e, o=x1_buf: e.tensor_add(o[:, :], dumpV[:, :],
                                                      BCA["b2"][:, :]),
                     after=[tx1a, tok_bca_b2,
                            t_t3[p - 2] if p >= 2 else None], inc=True)
            t_x1[p] = tx1
            tvb = ve(lambda e, a=x1sum, o=x1_buf: e.tensor_scalar(
                dumpV[:, :], o[:, :], 1.0, None, ALU.mult, op1=ALU.add,
                accum_out=a), after=[tx1], inc=True)
            tsb2 = se(lambda e, a=x1sq, o=x1_buf: e.activation(
                dumpS[:, :], o[:, :], AF.Square, accum_out=a),
                after=[tx1], inc=True)
            ve(lambda e, a=x1sum, m=x1m: e.tensor_scalar(m, a, 1.0 / D, None,
                                                         ALU.mult),
               after=[tvb])
            tw2 = ve(lambda e, a=x1sum, m=x1m, q=x1sq, w=x1w:
                     e.scalar_tensor_tensor(w, a, m, q, ALU.mult,
                                            ALU.subtract),
                     after=[tsb2], inc=True)
            tln2 = se(lambda e, w=x1w, s=x1sd: e.activation(
                s, w, AF.Ln, bias=ccst[:, 0:1], scale=ccst[:, 2:3]),
                after=[tw2], inc=True)
            trx2 = se(lambda e, s=x1sd, r=x1rstd: e.activation(
                r, s, AF.Exp, scale=ccst[:, 3:4]), after=[tln2], inc=True)
            ts1 = ve(lambda e, m=x1m, r=x1rstd, o=x1_buf: e.tensor_scalar(
                tp[4][:, :], o[:, :], m, r, ALU.subtract, ALU.mult),
                after=[trx2], inc=True)
            t_s1[p] = ts1
            th2c = ve(lambda e: e.tensor_mul(tp[5][:, :], tp[4][:, :],
                                             BCA["cb1"][:, :]),
                      after=[ts1, tok_bca_cb1], inc=True)
            h2cT_buf = keepT if j == nt - 1 else tdh
            th2cT = transpose("qs", "qtk" if j == nt - 1 else "qth",
                              h2cT_buf, tp[5],
                              after=[th2c] + ([modev[11]] if j == nt - 1
                                              else [t_mixk[p - 1]]))
            t_h2cT[p] = th2cT
            if j == nt - 1:
                tvh = ve(lambda e: e.tensor_tensor(
                    halos.ap().unsqueeze(-1), r8(keepT)[:, :, 127:128],
                    evenm.ap().unsqueeze(-1), ALU.mult),
                    after=[th2cT, tok_evenm], inc=True)
                thd = S.dma("qg", "qccH", dHin.ap(), halos[:, :], after=[tvh])
                tcc2 = S.collective([dHin.ap().opt()], [dHout.ap().opt()],
                                    groups, after=[thd])
                thr = S.dma("qg", "qccH", halos[:, :], dHout.ap(),
                            after=[tcc2])
                tb1 = ve(lambda e: e.tensor_mul(scr32[:, 0:8], halos[:, :],
                                                oddm[:, :]),
                         after=[thr, tok_oddm], inc=True)
                tb2 = ve(lambda e: e.tensor_mul(scr32[:, 8:16], COL["msbc"],
                                                evenm[:, :]),
                         after=[tb1, tok_msbc, tok_evenm], inc=True)
                tok_pc[0] = ve(lambda e: e.tensor_add(pcol[:, 0:8],
                                                      scr32[:, 0:8],
                                                      scr32[:, 8:16]),
                               after=[tb2], inc=True)
                t_mixk[p] = th2c
                return
            pcr = pcol[:, (j % 2) * 8:(j % 2) * 8 + 8]
            t1a = ve(lambda e: e.tensor_copy(
                r8(tp[1])[:, :, 1:128], r8(tdh)[:, :, 0:127]),
                after=[th2cT, tok_pc[j % 2],
                       t_kr[p - 2] if p >= 2 else None], inc=True)
            t1b = ve(lambda e, pc_=pcr: e.tensor_copy(
                r8(tp[1])[:, :, 0:1], pc_.unsqueeze(-1)), after=[t1a],
                inc=True)
            if j != nt - 1:
                pcw = pcol[:, ((j + 1) % 2) * 8:((j + 1) % 2) * 8 + 8]
                tok_pc[(j + 1) % 2] = ve(lambda e, pw=pcw: e.tensor_copy(
                    pw.unsqueeze(-1), r8(tdh)[:, :, 127:128]),
                    after=[th2cT], inc=True)
            td_ = ve(lambda e: e.tensor_sub(tp[5][:, :], tdh[:, :],
                                            tp[1][:, :]),
                     after=[t1b, th2cT], inc=True)
            tm1 = ve(lambda e: e.tensor_tensor(
                r8(dumpV), r8(tp[5]), murb, ALU.mult),
                after=[td_, tok_murc], inc=True)
            txr = ve(lambda e, o=xr_sl[p % 2]: e.tensor_add(
                o[:, :], dumpV[:, :], tp[1][:, :]),
                after=[tm1, t_r[p - 2] if p >= 2 else None], inc=True)
            t_mixr[p] = txr
            tm2 = ve(lambda e: e.tensor_tensor(
                r8(dumpV), r8(tp[5]), mukb, ALU.mult),
                after=[txr, tok_mukc], inc=True)
            txk = ve(lambda e, o=xk_sl[p % 2]: e.tensor_add(
                o[:, :], dumpV[:, :], tp[1][:, :]),
                after=[tm2, t_kf[p - 2] if p >= 2 else None], inc=True)
            t_mixk[p] = txk

        def p2_F_r(pf):
            t_r[pf] = proj(psB, xr_sl[pf % 2], lambda c: A1(1, c),
                           after=[t_mixr[pf], tok_w["WrT"], tok_sbr,
                                  t_kr[pf]],
                           bias_row=ROWB["sbr"])
            t_rsx[pf] = se(lambda e: e.activation(tp[6][:, :], psB[:, :],
                                                  AF.Exp, scale=ccst[:, 4:5]),
                           after=[t_r[pf], t_s1[pf + 1]], inc=True)
            ta1 = ve(lambda e: e.tensor_scalar(tp[6][:, :], tp[6][:, :], 1.0,
                                               None, ALU.add),
                     after=[t_rsx[pf]], inc=True)
            def _recip(e, nc=nc):
                with nc.allow_low_precision(reason="sigmoid tail, bf16 act"):
                    return e.reciprocal(tp[6][:, :], tp[6][:, :])
            t_rs[pf] = ve(_recip, after=[ta1], inc=True)

        def p2_F_kproj(pf, extra=()):
            t_kf[pf] = proj(psB, xk_sl[pf % 2], lambda c: A1(2, c),
                            after=[t_mixk[pf], tok_w["WkT"], tok_sbk,
                                   t_t3[pf - 1] if t_t3[pf - 1] is not None
                                   else tok_tks[nt - 1]] + list(extra),
                            bias_row=ROWB["sbk"])
            t_kr[pf] = ve(lambda e: e.tensor_single_scalar(
                tp[1][:, :], psB[:, :], 0.0, ALU.max),
                after=[t_kf[pf], t_mixk[pf + 1]], inc=True)
            t_ka[pf] = ve(lambda e: e.tensor_mul(tp[3][:, :], tp[1][:, :],
                                                 tp[1][:, :]),
                          after=[t_kr[pf], t_kaT[pf - 1]], inc=True)

        def p2_F_kaT(pf):
            t_kaT[pf] = transpose("qc", "qt2", td[0], tp[3],
                                  after=[t_ka[pf],
                                         t_vf[pf - 1] if t_vf[pf - 1]
                                         is not None else t_v2p[nt - 1]])

        def p2_F_v(pf, j, x1_buf):
            t_vf[pf] = proj(psB, td[0], lambda c: A2(1, c),
                            after=[t_kaT[pf], tok_wvt, t_rsx[pf]])
            t_t3[pf] = ve(lambda e: e.scalar_tensor_tensor(
                dumpV[:, :], psB[:, :], 1.0, tp[6][:, :], ALU.bypass,
                ALU.mult), after=[t_vf[pf], t_rs[pf]], inc=True)
            tx2 = ve(lambda e, o=x2b[pf % 2], xo=x1_buf: e.tensor_add(
                o[:, :], dumpV[:, :], xo[:, :]),
                after=[t_t3[pf], t_x1[pf], t_st[pf - 2] if pf >= 2 else None,
                       modev[11]],
                inc=True)
            t_st[pf] = S.dma("qg", "qst", yv[j], x2b[pf % 2][:, :],
                             after=[tx2])
            tok_store.append(t_st[pf])

        p2_P(0)
        p2_pref(0)
        for p in range(NP + 2):
            jq = order[p] if p < NP else None
            pf = p - 2
            jf = order[pf] if 0 <= pf < NP else None
            do_f = jf is not None and jf != nt - 1
            if jq is not None:
                p2_out(p)
            if p + 1 < NP:
                p2_pref(p + 1)
                p2_P(p + 1)
            if do_f:
                p2_F_kproj(pf)
                p2_F_r(pf)
            if jq is not None:
                p2_ao(p)
            if do_f:
                p2_F_kaT(pf)
                p2_F_v(pf, jf, x1_sl[pf % 2])
            if jq is not None:
                p2_R(p)
            if p == 1:
                emit_ffn_setup()

        jlast = nt - 1
        plast = NP + 2
        pcr = pcol[:, (jlast % 2) * 8:(jlast % 2) * 8 + 8]
        lastg = [t_t3[NP - 1], t_mixk[NP - 1], t_kr[NP - 1], t_rs[NP - 1]]
        t1a = ve(lambda e: e.tensor_copy(
            r8(tp[1])[:, :, 1:128], r8(keepT)[:, :, 0:127]),
            after=[t_h2cT[0], tok_pc[jlast % 2]] + lastg, inc=True)
        t1b = ve(lambda e: e.tensor_copy(
            r8(tp[1])[:, :, 0:1], pcr.unsqueeze(-1)), after=[t1a], inc=True)
        td_ = ve(lambda e: e.tensor_sub(tp[5][:, :], keepT[:, :], tp[1][:, :]),
                 after=[t1b], inc=True)
        tm1 = ve(lambda e: e.tensor_tensor(
            r8(dumpV), r8(tp[5]), murb, ALU.mult), after=[td_, tok_murc],
            inc=True)
        txr = ve(lambda e: e.tensor_add(xr_sl[0][:, :], dumpV[:, :],
                                        tp[1][:, :]),
                 after=[tm1, t_r[NP - 2]], inc=True)
        t_mixr[plast] = txr
        tm2 = ve(lambda e: e.tensor_tensor(
            r8(dumpV), r8(tp[5]), mukb, ALU.mult), after=[txr, tok_mukc],
            inc=True)
        txk = ve(lambda e: e.tensor_add(xk_sl[0][:, :], dumpV[:, :],
                                        tp[1][:, :]),
                 after=[tm2, t_kf[NP - 2]], inc=True)
        t_mixk[plast] = txk
        t_y[plast + 1] = txk
        t_s1[plast + 1] = txk
        t_mixk[plast + 1] = txk
        t_vf[plast - 1] = t_vf[NP - 1]
        p2_F_r(plast)
        p2_F_kproj(plast, extra=[t_rs[plast]])
        p2_F_kaT(plast)
        p2_F_v(plast, jlast, keepX)

        finals = {}
        for (s, v) in tok_store:
            finals[s] = max(finals.get(s, 0), v)
        for s, v in finals.items():
            S.ops["gpsimd"].append(Op(lambda e, ss=s, vv=v:
                                      e.wait_ge(sems[ss], vv), (), None, 1))

        with nc.Block() as block:
            S.emit(block)

    return nc


_NC_CACHE = {}

def _get_nc(ntiles=16):
    if ntiles not in _NC_CACHE:
        _NC_CACHE[ntiles] = build_nc(ntiles)
    return _NC_CACHE[ntiles]


def kernel(**inputs):
    L = int(inputs["L0_attn_x"].shape[1])
    toks = L // 2
    ntiles = toks // 128
    nc = _get_nc(ntiles)
    x = np.ascontiguousarray(np.asarray(inputs["L0_attn_x"], np.float32))
    row = lambda k: np.ascontiguousarray(np.asarray(inputs[k],
                                                    np.float32)).reshape(1, D)
    base = {
        "Wq": np.ascontiguousarray(np.asarray(inputs["L0_attn_Wq"], np.float32)),
        "Wk": np.ascontiguousarray(np.asarray(inputs["L0_attn_Wk"], np.float32)),
        "Wv": np.ascontiguousarray(np.asarray(inputs["L0_attn_Wv"], np.float32)),
        "Wq2": np.ascontiguousarray(np.asarray(inputs["L0_attn_Wq2"],
                                               np.float32)),
        "Wk2": np.ascontiguousarray(np.asarray(inputs["L0_attn_Wk2"],
                                               np.float32)),
        "Wv2": np.ascontiguousarray(np.asarray(inputs["L0_attn_Wv2"],
                                               np.float32)),
        "Wo": np.ascontiguousarray(np.asarray(inputs["L0_attn_Wo"], np.float32)),
        "WrT": np.ascontiguousarray(np.asarray(inputs["L0_ffn_W_r"],
                                               np.float32).T),
        "WkT": np.ascontiguousarray(np.asarray(inputs["L0_ffn_W_k"],
                                               np.float32).T),
        "WvT": np.ascontiguousarray(np.asarray(inputs["L0_ffn_W_v"],
                                               np.float32).T),
        "wmod": np.ascontiguousarray(np.asarray(inputs["L0_wmod"], np.float32)),
        "bq": row("L0_attn_bq"), "bk": row("L0_attn_bk"), "bv": row("L0_attn_bv"),
        "bq2": row("L0_attn_bq2"), "bk2": row("L0_attn_bk2"),
        "bv2": row("L0_attn_bv2"), "bo": row("L0_attn_bo"),
        "gamma": row("L0_attn_gamma"), "beta": row("L0_attn_beta"),
        "mu_r": row("L0_ffn_mu_r"), "mu_k": row("L0_ffn_mu_k"),
    }
    cmat = np.ascontiguousarray(np.asarray(inputs["L0_c"], np.float32))
    in_maps = []
    for core in range(8):
        b = core // 2
        half = core % 2
        m = dict(base)
        m["x"] = np.ascontiguousarray(x[b, half * toks:(half + 1) * toks, :])
        m["crow"] = cmat[b].reshape(1, D).copy()
        m["oddm"] = np.full((128, 8), float(half), np.float32)
        in_maps.append(m)
    res = run_bass_kernel_spmd(nc, in_maps, core_ids=list(range(8)))
    out = np.empty((4, L, D), np.float32)
    for core in range(8):
        b = core // 2
        half = core % 2
        out[b, half * toks:(half + 1) * toks, :] = res.results[core]["y"]
    return out


if __name__ == "__main__":
    ncb = build_nc(2)
    print("built", ncb.n_instructions(), "instructions")


# version 22
# speedup vs baseline: 1.0743x; 1.0737x over previous
import numpy as np
import concourse.bass as bass
import concourse.mybir as mybir
from concourse.bass_utils import run_bass_kernel_spmd

F32 = mybir.dt.float32
BF16 = mybir.dt.bfloat16
AF = mybir.ActivationFunctionType
ALU = mybir.AluOpType

D = 1024
NH = 16
NC = 8
EPS_LN_OUT = 1e-5
EPS_LN_ATTN = 1e-3
EPS_QK = 1e-6


class Op:
    __slots__ = ("fn", "waits", "inc", "step", "selfwait")
    def __init__(self, fn, waits, inc, step, selfwait=None):
        self.fn, self.waits, self.inc, self.step = fn, waits, inc, step
        self.selfwait = selfwait


class Sched:
    ENGINES = ("tensor", "vector", "scalar", "gpsimd", "sync")

    QENG = {"qg": "gpsimd", "qs": "sync", "qc": "scalar"}

    def __init__(self, nc, sems):
        self.tlag = 2
        self.nc = nc
        self.sems = sems
        self.cnt = {k: 0 for k in sems}
        self.ops = {e: [] for e in self.ENGINES}

    def _upgrade(self, after):
        out = []
        for t in after:
            if not t:
                continue
            sname, v = t
            if not sname.startswith("e_") and sname != "cc":
                v = self.cnt[sname]
            out.append((sname, v))
        return tuple(out)

    def op(self, engine, fn, after=(), inc=True):
        incname = "e_" + engine
        prev = self.cnt[incname]
        self.cnt[incname] += 1
        tok = (incname, self.cnt[incname])
        lag = self.tlag if engine == "tensor" else 0
        self.ops[engine].append(Op(fn, self._upgrade(after), incname, 1,
                                   selfwait=(incname, max(prev - lag, 0))))
        return tok

    def dma(self, q, sem, out, in_, after=(), transpose=False):
        eng = self.QENG[q]
        waits = self._upgrade(after)
        self.cnt[sem] += 16
        tok = (sem, self.cnt[sem])
        if transpose:
            fn = lambda e, o=out, i=in_: e.dma_start(out=o, in_=i, transpose=True)
        else:
            fn = lambda e, o=out, i=in_: e.dma_start(out=o, in_=i)
        self.ops[eng].append(Op(fn, waits, sem, 16))
        return tok

    def collective(self, ins, outs, groups, after=()):
        self.cnt["cc"] += 1
        tok = ("cc", self.cnt["cc"])
        def fn(e, i=ins, o=outs, g=groups):
            return e.collective_compute("AllReduce", ALU.add, replica_groups=g,
                                        ins=i, outs=o)
        self.ops["gpsimd"].append(Op(fn, self._upgrade(after), "cc", 1))
        return tok

    def emit(self, block):
        binders = {"tensor": block.tensor, "vector": block.vector,
                   "scalar": block.scalar, "gpsimd": block.gpsimd,
                   "sync": block.sync}
        for ename in self.ENGINES:
            ops = self.ops[ename]
            sems = self.sems
            def body(eng, ops=ops, sems=sems, ename=ename):
                last = {}
                for o in ops:
                    if o.selfwait is not None:
                        last[o.selfwait[0]] = max(last.get(o.selfwait[0], 0),
                                                  o.selfwait[1])
                    for (s, v) in o.waits:
                        if last.get(s, 0) < v:
                            eng.wait_ge(sems[s], v)
                            last[s] = v
                    inst = o.fn(eng)
                    if o.selfwait is not None:
                        inst._wait_ge(sems[o.selfwait[0]], o.selfwait[1])
                    if o.inc is not None:
                        inst.then_inc(sems[o.inc], o.step)
            binders[ename](body)


def bc_last(ap2d, n):
    return ap2d.unsqueeze(-1).broadcast_to(tuple(list(ap2d.shape) + [n]))


def _ap(x):
    return x if isinstance(x, bass.AP) else x.ap()


def r8(x):
    return _ap(x).rearrange("p (c t) -> p c t", c=NC)


def build_nc(ntiles=16):
    nc = bass.Bass("TRN2", target_bir_lowering=False,
                   detect_race_conditions=False)
    nt = ntiles
    tok_n = nt * 128

    dx = nc.dram_tensor("x", [tok_n, D], F32, kind="ExternalInput")
    dy = nc.dram_tensor("y", [tok_n, D], F32, kind="ExternalOutput")
    dW = {w: nc.dram_tensor(w, [D, D], F32, kind="ExternalInput")
          for w in ("Wq", "Wk", "Wv", "Wq2", "Wk2", "Wv2", "Wo",
                    "WrT", "WkT", "WvT")}
    dwmod = nc.dram_tensor("wmod", [D, 6 * D], F32, kind="ExternalInput")
    drow = {r: nc.dram_tensor(r, [1, D], F32, kind="ExternalInput")
            for r in ("bq", "bk", "bv", "bq2", "bk2", "bv2", "bo", "gamma",
                      "beta", "mu_r", "mu_k", "crow")}
    doddm = nc.dram_tensor("oddm", [128, 8], F32, kind="ExternalInput")
    dhc = nc.dram_tensor("hcspill", [tok_n, D], BF16)
    dv2 = nc.dram_tensor("v2spill", [tok_n, D], BF16)
    dSin = nc.dram_tensor("ccSin", [128, 512], F32)
    dSout = nc.dram_tensor("ccSout", [128, 512], F32)
    dHin = nc.dram_tensor("ccHin", [128, 8], F32)
    dHout = nc.dram_tensor("ccHout", [128, 8], F32)

    xv = dx.ap().rearrange("(i p) n -> i p n", p=128)
    yv = dy.ap().rearrange("(i p) n -> i p n", p=128)
    hcv = dhc.ap().rearrange("(i p) n -> i p n", p=128)
    v2v = dv2.ap().rearrange("(i p) n -> i p n", p=128)
    wv = {k: v.ap().rearrange("(c p) n -> c p n", p=128) for k, v in dW.items()}
    wmv = dwmod.ap().rearrange("(c p) n -> c p n", p=128)
    groups = [[0, 1], [2, 3], [4, 5], [6, 7]]

    import contextlib
    stack = contextlib.ExitStack()
    with stack:
        sb = lambda name, shape, dt, **kw: stack.enter_context(
            nc.sbuf_tensor(name, shape, dt, **kw))
        pst = lambda name, shape: stack.enter_context(
            nc.psum_tensor(name, shape, F32))

        arena1 = sb("arena1", [128, 4 * NC * D], BF16)
        arena2 = sb("arena2", [128, 2 * NC * D], BF16)
        wmodb = sb("wmodb", [128, 2 * NC * 512], BF16,
                   is_transpose_dest=True)
        bca = sb("bca", [128, 5 * D], BF16)
        Sbf = sb("Sbf", [128, 1024], BF16)
        recip = sb("recip", [128, nt * 16], F32)
        stats = sb("stats", [128, 64], F32)
        ccst = sb("ccst", [128, 5], F32)
        xr_sl = [sb(f"xrT{i}", [128, D], BF16) for i in range(2)]
        xk_sl = [sb(f"xkT{i}", [128, D], BF16) for i in range(2)]
        x1_sl = [sb(f"x1b{i}", [128, D], BF16) for i in range(2)]
        cols = sb("cols", [128, 64], BF16)
        pcol = sb("pcol", [128, 16], BF16)
        oddm = sb("oddm_s", [128, 8], F32)
        evenm = sb("evenm", [128, 8], F32)
        halos = sb("halos", [128, 8], F32)
        rows = sb("rows", [1, 8 * D], BF16)
        rowb = sb("rowb", [1, 6 * D], BF16)
        hvar = sb("hvar", [1, D], F32)
        onesr = sb("onesr", [1, 128], BF16)
        one1 = sb("one1", [1, 1], BF16)
        xb = [sb(f"xb{i}", [128, D], BF16) for i in range(2)]
        hcb = [sb(f"hcb{i}", [128, D], BF16) for i in range(2)]
        scr32 = sb("scr32", [128, D], BF16)
        dumpV = sb("dumpV", [128, D], BF16)
        dumpS = sb("dumpS", [128, D], BF16)
        tp = [sb(f"tp{i}", [128, D], BF16) for i in range(7)]
        td = [sb(f"td{i}", [128, D], BF16, is_transpose_dest=True) for i in range(4)]
        tdh = sb("tdh", [128, D], BF16, is_transpose_dest=True)
        keepT = wmodb[:, 0:D]
        keepX = wmodb[:, D:2 * D]
        x2b = [wmodb[:, 2 * D:3 * D], wmodb[:, 3 * D:4 * D]]

        psA = pst("psA", [128, D])
        psB = pst("psB", [128, D])
        psC = pst("psC", [128, D])
        psS = pst("psS", [128, 512])
        psX = pst("psX", [128, 512])

        semnames = (["e_tensor", "e_vector", "e_scalar", "e_gpsimd", "e_sync",
                     "cc", "qx0", "qx1", "qw0", "qw1", "qw2", "qw3", "qa0",
                     "qa1", "qwm0", "qwm1", "qt0", "qt1", "qt2", "qt3", "qtk",
                     "qhcs", "qhcr", "qst", "qccS", "qccH", "qr0", "qr1",
                     "qodd", "qth", "qv2", "qwt"])
        sems = {n: stack.enter_context(nc.semaphore(n)) for n in semnames}
        S = Sched(nc, sems)

        A1 = lambda j, c: arena1[:, (j * NC + c) * D:(j * NC + c + 1) * D]
        A2 = lambda j, c: arena2[:, (j * NC + c) * D:(j * NC + c + 1) * D]
        BCA = {k: bca[:, i * D:(i + 1) * D] for i, k in
               enumerate(("ca1", "cb1", "sabo", "g2", "b2"))}
        ROW = {k: rows[:, i * D:(i + 1) * D] for i, k in enumerate(
            ("ca1", "cb1", "sa", "ga", "sb", "gb", "t5", "t6"))}
        ROWB = {k: rowb[:, i * D:(i + 1) * D] for i, k in enumerate(
            ("bq2f", "bk2f", "bv2f", "sbr", "sbk", "sabo"))}
        HROW = {}
        COL = {k: cols[:, i * 8:(i + 1) * 8] for i, k in enumerate(
            ("siluc", "sac", "murc", "mukc", "msbc", "sbc", "scr"))}

        te = lambda fn, after=(), inc=False: S.op("tensor", fn, after, inc)
        ve = lambda fn, after=(), inc=False: S.op("vector", fn, after, inc)
        se = lambda fn, after=(), inc=False: S.op("scalar", fn, after, inc)

        psx_chain = [None]

        def proj(psum, lhsT_buf, wslot_fn, after, bias_row=None):
            t = None
            for c in range(NC):
                lhs = lhsT_buf[:, c * 128:(c + 1) * 128]
                w = wslot_fn(c)
                st = (c == 0)
                sp = (c == NC - 1) and bias_row is None
                aft = after if c == 0 else ()
                te(lambda e, o=psum[:, 0:512], l=lhs, r=w[:, 0:512], s=st, p=sp:
                   e.matmul(o, l, r, start=s, stop=p), after=aft)
                t = te(lambda e, o=psum[:, 512:1024], l=lhs, r=w[:, 512:1024],
                       s=st, p=sp: e.matmul(o, l, r, start=s, stop=p),
                       inc=(bias_row is None and c == NC - 1))
            if bias_row is not None:
                te(lambda e, o=psum[:, 0:512], r=bias_row[0:1, 0:512]:
                   e.matmul(o, onesr[0:1, :], r, start=False, stop=True))
                t = te(lambda e, o=psum[:, 512:1024], r=bias_row[0:1, 512:1024]:
                       e.matmul(o, onesr[0:1, :], r, start=False, stop=True),
                       inc=True)
            return t

        def transpose(q, sem, dst, src, after):
            return S.dma(q, sem, r8(dst), r8(src), after=after, transpose=True)

        def row_to_cols(dst_col8, src_row, after):
            psXb = psX.ap().bitcast(BF16)
            tlast = None
            for c in range(NC):
                aft = list(after) + [psx_chain[0]] if c == 0 else ()
                tlast = te(lambda e, o=psXb[:, 2 * c:2 * c + 1],
                           i=src_row[0:1, c * 128:(c + 1) * 128]:
                           e.transpose(o, i, one1[0:1, 0:1]),
                           after=aft, inc=(c == NC - 1))
            src = psXb[:, 0:16].rearrange("p (h two) -> p h two", two=2)[:, :, 0]
            tok = ve(lambda e: e.tensor_copy(dst_col8, src),
                     after=[tlast], inc=True)
            psx_chain[0] = tok
            return tok

        def bcast_row2(dst_tile, src_row_bf, after):
            tok = None
            for h in range(2):
                sl = slice(h * 512, (h + 1) * 512)
                aft = list(after) + [psx_chain[0]] if h == 0 else [tok]
                t = te(lambda e, s=sl: e.matmul(psX[:, 0:512], onesr[0:1, :],
                                                src_row_bf[0:1, s],
                                                start=True, stop=True),
                       after=aft, inc=True)
                tok = ve(lambda e, s=sl: e.tensor_copy(dst_tile[:, s],
                                                       psX[:, 0:512]),
                         after=[t], inc=True)
            psx_chain[0] = tok
            return tok

        def mm8(colsrc, wslot_fn, off, after):
            t = None
            for c in range(NC):
                t = te(lambda e, o=psX[0:1, 0:512], l=colsrc[:, c:c + 1],
                       r=wslot_fn(c)[:, off:off + 512], s=(c == 0),
                       p=(c == NC - 1): e.matmul(o, l, r, start=s, stop=p),
                       after=after if c == 0 else (), inc=(c == NC - 1))
            return t

        hv_slots = [hvar[0:1, 0:D]]
        hv_guard = [[]]
        hv_i = [0]
        tok_hrow = {}
        def hload(name):
            sidx = 0
            hv_i[0] += 1
            ap = hv_slots[sidx]
            tok = S.dma("qs", f"qr{sidx}", ap, drow[name].ap(),
                        after=list(hv_guard[sidx]))
            hv_guard[sidx] = []
            HROW[name] = ap
            tok_hrow[name] = tok
            return ap, tok, sidx
        def hused(sidx, tok):
            hv_guard[sidx].append(tok)
        tok_oddm = S.dma("qs", "qodd", oddm[:, :], doddm.ap())
        ve(lambda e: e.memset(onesr[0:1, :], 1.0))
        ve(lambda e: e.memset(one1[0:1, :], 1.0))
        ve(lambda e: e.memset(ccst[:, 0:1], EPS_LN_OUT))
        ve(lambda e: e.memset(ccst[:, 1:2], EPS_LN_ATTN))
        ve(lambda e: e.memset(ccst[:, 2:3], -1.0 / D))
        ve(lambda e: e.memset(ccst[:, 3:4], -0.5))
        ve(lambda e: e.memset(ccst[:, 4:5], -1.0))
        tok_ones = ve(lambda e: e.memset(COL["scr"], 0.0), inc=True)
        tok_evenm = ve(lambda e: e.tensor_scalar(evenm[:, :], oddm[:, :], -1.0, 1.0,
                                                 ALU.mult, ALU.add),
                       after=[tok_oddm], inc=True)
        crow_ap, tok_crow, crow_s = hload("crow")
        tok_silu = se(lambda e: e.activation(ROW["t5"][0:1, :], crow_ap,
                                             AF.Silu), after=[tok_crow],
                      inc=True)
        hused(crow_s, tok_silu)
        tok_siluc = row_to_cols(COL["siluc"], ROW["t5"], [tok_silu, tok_ones])

        tok_w = {}
        tok_wc = {}
        def load_w(name, arena_fn, slot, after=()):
            sem = (f"qw{slot}" if arena_fn is A1 else f"qa{slot}")
            toks = []
            for c in range(NC):
                toks.append(S.dma("qg", sem, arena_fn(slot, c), wv[name][c],
                                  after=after if c == 0 else ()))
            tok_w[name] = toks[-1]
            tok_wc[name] = toks
        load_w("Wv", A2, 0)
        load_w("Wv2", A2, 1)

        tok_x = {}
        for i in range(min(2, nt)):
            tok_x[i] = S.dma("qg", f"qx{i % 2}", xb[i][:, :], xv[i])

        prod_mmlast = {}
        prodtok_td = {None: None}
        prod_ev = {0: None, 1: None}
        for c in range(NC):
            tt_c = S.dma("qs", "qwt", r8(td[c % 2]), r8(A2(0, c)),
                         after=[tok_wc["Wv"][c], prod_mmlast.get(c - 2)],
                         transpose=True)
            for h in range(2):
                t = None
                for e_ in range(NC):
                    aft = ([tt_c, prod_ev[h], tok_w["Wv2"]]
                           if e_ == 0 else ())
                    t = te(lambda e, o=psC[:, h * 512:(h + 1) * 512],
                           l=td[c % 2][:, e_ * 128:(e_ + 1) * 128],
                           r=A2(1, e_)[:, h * 512:(h + 1) * 512],
                           s=(e_ == 0), p=(e_ == NC - 1):
                           e.matmul(o, l, r, start=s, stop=p),
                           after=aft, inc=(e_ == NC - 1))
                prod_mmlast[c] = t
                prodtok_td[c % 2] = t
                prod_ev[h] = ve(lambda e, d=A2(0, c)[:, h * 512:(h + 1) * 512],
                                p=psC[:, h * 512:(h + 1) * 512]:
                                e.tensor_copy(d, p), after=[t], inc=True)
        tok_Wvv_mm = prod_mmlast[NC - 1]
        tok_Wvv_ev = prod_ev[1]

        WM = lambda half, c: wmodb[:, (half * NC + c) * 512:(half * NC + c + 1) * 512]
        tok_wm = {}
        modev = {}
        blknames = ("sa", "ca1", "ga", "sb", "cb1", "gb")
        tok_mod = {}
        for sbi in range(12):
            half = sbi % 2
            for c in range(NC):
                tok_wm[(sbi, c)] = S.dma(
                    "qg", f"qwm{half}", WM(half, c),
                    wmv[c][:, sbi * 512:(sbi + 1) * 512],
                    after=[modev.get(sbi - 2)])
            pap = (psA if half == 0 else psB)[0:1, 0:512]
            t = None
            for c in range(NC):
                aft = [tok_siluc, tok_wm[(sbi, c)], modev.get(sbi - 2)] if c == 0 \
                    else [tok_wm[(sbi, c)]]
                t = te(lambda e, o=pap, l=COL["siluc"][:, c:c + 1],
                       r=WM(half, c), s=(c == 0), p=(c == NC - 1):
                       e.matmul(o, l, r, start=s, stop=p),
                       after=aft, inc=(c == NC - 1))
            blk = blknames[sbi // 2]
            off = (sbi % 2) * 512
            tv = ve(lambda e, d=ROW[blk][0:1, off:off + 512], p=pap:
                    e.tensor_copy(d, p), after=[t], inc=True)
            modev[sbi] = tv
            tok_mod[blk] = tv
        mod_psA_last = modev[10]
        mod_psB_last = modev[11]

        rowop = lambda fn, after: ve(fn, after=after, inc=True)
        tok_ca1 = rowop(lambda e: e.tensor_scalar(ROW["ca1"][0:1, :],
                                                  ROW["ca1"][0:1, :], 1.0, None,
                                                  ALU.add), [tok_mod["ca1"]])
        tok_cb1 = rowop(lambda e: e.tensor_scalar(ROW["cb1"][0:1, :],
                                                  ROW["cb1"][0:1, :], 1.0, None,
                                                  ALU.add), [tok_mod["cb1"]])

        def clamp_row(name, after):
            src = ROW[name]
            t1 = rowop(lambda e: e.tensor_scalar(ROW["t5"][0:1, :], src[0:1, :],
                                                 0.0, None, ALU.is_ge), after)
            t2 = rowop(lambda e: e.tensor_scalar(ROW["t5"][0:1, :],
                                                 ROW["t5"][0:1, :], 2.0, 1.0,
                                                 ALU.mult, ALU.subtract), [t1])
            t3a = se(lambda e: e.activation(ROW["t6"][0:1, :], src[0:1, :],
                                            AF.Abs), after=[t2], inc=True)
            t3 = rowop(lambda e: e.tensor_scalar(ROW["t6"][0:1, :],
                                                 ROW["t6"][0:1, :],
                                                 0.25, None, ALU.max), [t3a])
            return rowop(lambda e: e.tensor_mul(src[0:1, :], ROW["t5"][0:1, :],
                                                ROW["t6"][0:1, :]), [t3])
        tok_gac = clamp_row("ga", [tok_mod["ga"], tok_siluc])
        tok_gbc = clamp_row("gb", [tok_mod["gb"], tok_gac])

        def bcast_from_f32(dst_tile, build_fn, after):
            tb = rowop(build_fn, after)
            return bcast_row2(dst_tile, ROW["t5"], [tb])

        tok_bca_ca1 = bcast_from_f32(
            BCA["ca1"], lambda e: e.tensor_copy(ROW["t5"][0:1, :],
                                                ROW["ca1"][0:1, :]), [tok_ca1])
        tok_bca_cb1 = bcast_from_f32(
            BCA["cb1"], lambda e: e.tensor_copy(ROW["t5"][0:1, :],
                                                ROW["cb1"][0:1, :]), [tok_cb1])
        gam_ap, tok_gam, gam_s = hload("gamma")
        tok_bca_g2 = bcast_from_f32(
            BCA["g2"], lambda e: e.tensor_mul(ROW["t5"][0:1, :], ROW["ga"][0:1, :],
                                              gam_ap),
            [tok_gac, tok_gam])
        hused(gam_s, tok_bca_g2)
        bet_ap, tok_bet, bet_s = hload("beta")
        tok_bca_b2 = bcast_from_f32(
            BCA["b2"], lambda e: e.tensor_mul(ROW["t5"][0:1, :], ROW["ga"][0:1, :],
                                              bet_ap),
            [tok_gac, tok_bet])
        hused(bet_s, tok_bca_b2)
        bo_ap, tok_bo, bo_s = hload("bo")
        tok_sabo_row = rowop(lambda e: e.tensor_add(ROW["t5"][0:1, :],
                                                    ROW["sa"][0:1, :], bo_ap),
                             [tok_mod["sa"], tok_bo, tok_bca_b2])
        hused(bo_s, tok_sabo_row)
        tok_saborb = rowop(lambda e: e.tensor_copy(ROWB["sabo"],
                                                   ROW["t5"][0:1, :]),
                           [tok_sabo_row])
        tok_bca_sabo = bcast_row2(BCA["sabo"], ROWB["sabo"], [tok_saborb])

        tok_sac = row_to_cols(COL["sac"], ROW["sa"], [tok_mod["sa"]])
        mur_ap, tok_mur, mur_s = hload("mu_r")
        t_mr = rowop(lambda e: e.tensor_copy(ROW["t5"][0:1, :], mur_ap),
                     [tok_mur, tok_bca_sabo])
        hused(mur_s, t_mr)
        tok_murc = row_to_cols(COL["murc"], ROW["t5"], [t_mr])
        muk_ap, tok_muk, muk_s = hload("mu_k")
        t_mk = rowop(lambda e: e.tensor_copy(ROW["t5"][0:1, :], muk_ap),
                     [tok_muk, tok_murc])
        hused(muk_s, t_mk)
        tok_mukc = row_to_cols(COL["mukc"], ROW["t5"], [t_mk])
        t_ms = rowop(lambda e: e.tensor_scalar(ROW["t5"][0:1, :], ROW["sb"][0:1, :],
                                               -1.0, None, ALU.mult),
                     [tok_mod["sb"], tok_mukc])
        tok_msbc = row_to_cols(COL["msbc"], ROW["t5"], [t_ms])
        tok_sbc = row_to_cols(COL["sbc"], ROW["sb"], [tok_mod["sb"]])

        load_w("Wq", A1, 0)
        load_w("Wk", A1, 1)
        load_w("Wq2", A1, 2)
        load_w("Wk2", A1, 3)

        def fold_chain(wfn1, b1name, wfn2, b2name, dstb, after):
            b1_ap, tok_b1, b1_s = hload(b1name)
            tok = None
            for half in range(2):
                off = half * 512
                t = mm8(COL["sac"], wfn1, off,
                        (list(after) + [psx_chain[0]]) if half == 0 else [tok])
                tok = ve(lambda e, o=off: e.scalar_tensor_tensor(
                    ROW["t6"][0:1, o:o + 512], psX[0:1, 0:512], 1.0,
                    b1_ap[0:1, o:o + 512], ALU.mult, ALU.add),
                    after=[t, tok_b1], inc=True)
            hused(b1_s, tok)
            psx_chain[0] = tok
            b2_ap, tok_b2, b2_s = hload(b2name)
            tcols = row_to_cols(COL["scr"], ROW["t6"], [tok])
            for half in range(2):
                off = half * 512
                t = mm8(COL["scr"], wfn2, off,
                        [tcols, psx_chain[0]] if half == 0 else [tok])
                tok = ve(lambda e, o=off: e.scalar_tensor_tensor(
                    dstb[0:1, o:o + 512], psX[0:1, 0:512], 1.0,
                    b2_ap[0:1, o:o + 512], ALU.mult, ALU.add),
                    after=[t, tok_b2], inc=True)
            hused(b2_s, tok)
            psx_chain[0] = tok
            return tok

        tok_bq2f = fold_chain(lambda c: A1(0, c), "bq", lambda c: A1(2, c), "bq2",
                              ROWB["bq2f"], [tok_sac, tok_w["Wq"], tok_w["Wq2"]])
        tok_bk2f = fold_chain(lambda c: A1(1, c), "bk", lambda c: A1(3, c), "bk2",
                              ROWB["bk2f"], [tok_w["Wk"], tok_w["Wk2"]])

        bv_ap, tok_bv, bv_s = hload("bv")
        tcpv = rowop(lambda e: e.tensor_copy(ROW["t5"][0:1, :], bv_ap),
                     [tok_bv, tok_msbc])
        hused(bv_s, tcpv)
        tcolsv = row_to_cols(COL["scr"], ROW["t5"], [tcpv])
        bv2_ap, tok_bv2r, bv2_s = hload("bv2")
        tokf = None
        tok_f1v = None
        for half in range(2):
            off = half * 512
            t = mm8(COL["scr"], lambda c: A2(1, c), off,
                    [tcolsv, psx_chain[0], tok_w["Wv2"]] if half == 0
                    else [tokf])
            tok_f1v = t
            tokf = ve(lambda e, o=off: e.scalar_tensor_tensor(
                ROW["t6"][0:1, o:o + 512], psX[0:1, 0:512], 1.0,
                bv2_ap[0:1, o:o + 512], ALU.mult, ALU.add),
                after=[t, tok_bv2r], inc=True)
        hused(bv2_s, tokf)
        psx_chain[0] = tokf
        for half in range(2):
            off = half * 512
            t = mm8(COL["sac"], lambda c: A2(0, c), off,
                    [tokf, tok_sac, tok_Wvv_ev] if half == 0 else [tokf])
            tokf = ve(lambda e, o=off: e.scalar_tensor_tensor(
                ROWB["bv2f"][0:1, o:o + 512], psX[0:1, 0:512], 1.0,
                ROW["t6"][0:1, o:o + 512], ALU.mult, ALU.add),
                after=[t], inc=True)
        psx_chain[0] = tokf
        tok_bv2f = tokf
        load_w("WvT", A2, 1, after=[tok_f1v, tok_Wvv_mm])

        tok_hc = [None] * nt
        tok_spill = [None] * nt
        tok_hcT = [None] * nt
        tok_recip = [None] * nt
        tok_xread = [None] * nt
        tok_sq = [None] * nt
        tok_S_pertile = [None] * nt
        tok_qe = [None] * nt
        tok_ke = [None] * nt
        tok_tqs = [None] * nt
        tok_tks = [None] * nt
        tok_tq2s = [None] * nt
        tok_tk2s = [None] * nt
        tok_tqT = [None] * nt
        tok_tkT = [None] * nt
        tok_tq2 = [None] * nt
        tok_tk2 = [None] * nt
        tok_tek = [None] * nt
        g_psA = [mod_psA_last]
        g_psB = [mod_psB_last]
        g_psC = [tok_Wvv_ev]
        g_td0 = [prodtok_td[0]]
        t_v2p = [None] * nt
        t_v2ev = [None] * nt
        t_v2st = [None] * nt

        def p1_A(i):
            st = stats[:, (i % 2) * 16:(i % 2) * 16 + 16]
            xsum, xsq, xm, w_, sd, rstd = (st[:, k:k + 1] for k in range(6))
            if i not in tok_x:
                tok_x[i] = S.dma("qg", f"qx{i % 2}", xb[i % 2][:, :], xv[i],
                                 after=[tok_xread[i - 2], tok_sq[i - 2]])
            xt = xb[i % 2]
            tva = ve(lambda e, x=xt, a=xsum: e.tensor_scalar(
                dumpV[:, :], x[:, :], 1.0, None, ALU.mult, op1=ALU.add,
                accum_out=a), after=[tok_x[i]], inc=True)
            tsa = se(lambda e, x=xt, a=xsq: e.activation(
                dumpS[:, :], x[:, :], AF.Square, accum_out=a),
                after=[tok_x[i], tok_xread[i - 2] if i >= 2 else None],
                inc=True)
            tok_sq[i] = tsa
            ve(lambda e, a=xsum, m=xm: e.tensor_scalar(m, a, 1.0 / D, None,
                                                       ALU.mult), after=[tva])
            tw = ve(lambda e, a=xsum, m=xm, q=xsq, w=w_: e.scalar_tensor_tensor(
                w, a, m, q, ALU.mult, ALU.subtract), after=[tsa], inc=True)
            tln = se(lambda e, w=w_, s=sd: e.activation(
                s, w, AF.Ln, bias=ccst[:, 0:1], scale=ccst[:, 2:3]),
                after=[tw, tok_ones], inc=True)
            trx = se(lambda e, s=sd, r=rstd: e.activation(
                r, s, AF.Exp, scale=ccst[:, 3:4]), after=[tln], inc=True)
            tok_xread[i] = tt_ = ve(lambda e, x=xt, m=xm, r=rstd:
                                    e.tensor_scalar(dumpV[:, :], x[:, :], m, r,
                                                    ALU.subtract, ALU.mult),
                                    after=[trx], inc=True)
            tok_hc[i] = ve(lambda e, o=hcb[i % 2]: e.tensor_mul(
                o[:, :], dumpV[:, :], BCA["ca1"][:, :]),
                after=[tt_, tok_bca_ca1,
                       tok_spill[i - 2] if i >= 2 else None,
                       tok_hcT[i - 2] if i >= 2 else None], inc=True)
            tok_spill[i] = S.dma("qs", "qhcs", hcv[i], hcb[i % 2][:, :],
                                 after=[tok_hc[i]])
            tok_hcT[i] = transpose("qs", "qt0", td[0], hcb[i % 2],
                                   after=[tok_hc[i], g_td0[0]])

        def p1_B(i):
            tq = proj(psA, td[0], lambda c: A1(0, c),
                      after=[tok_hcT[i], tok_w["Wq"], g_psA[0]])
            tk = proj(psB, td[0], lambda c: A1(1, c),
                      after=[tok_w["Wk"], g_psB[0]])
            tv2 = proj(psC, td[0], lambda c: A2(0, c),
                       after=[tok_bv2f, g_psC[0]], bias_row=ROWB["bv2f"])
            t_v2p[i] = tv2
            g_td0[0] = tv2
            tok_tqs[i] = ve(lambda e: e.tensor_copy(tp[0][:, :], psA[:, :]),
                            after=[tq], inc=True)
            tok_tks[i] = se(lambda e: e.copy(tp[1][:, :], psB[:, :]),
                            after=[tk, tok_ke[i - 1] if i >= 1 else None],
                            inc=True)
            g_psB[0] = tok_tks[i]
            t_v2ev[i] = se(lambda e: e.copy(tp[6][:, :], psC[:, :]),
                           after=[tv2, t_v2st[i - 1] if i >= 1 else None],
                           inc=True)
            g_psC[0] = t_v2ev[i]
            tok_tqT[i] = transpose("qs", "qt1", td[1], tp[0],
                                   after=[tok_tqs[i], tok_tq2[i - 1]
                                          if i >= 1 else prodtok_td[1]])
            tok_tkT[i] = transpose("qc", "qt2", td[2], tp[1],
                                   after=[tok_tks[i], tok_tk2[i - 1]
                                          if i >= 1 else None])
            t_v2st[i] = S.dma("qs", "qv2", v2v[i], tp[6][:, :],
                              after=[t_v2ev[i]])

        def p1_C(i):
            st = stats[:, (i % 2) * 16:(i % 2) * 16 + 16]
            qksl = stats[:, 32 + (i % 2) * 16:32 + (i % 2) * 16 + 16]
            tq2 = proj(psC, td[1], lambda c: A1(2, c),
                       after=[tok_tqT[i], tok_w["Wq2"], tok_bq2f, g_psC[0]],
                       bias_row=ROWB["bq2f"])
            tok_tq2[i] = tq2
            tk2 = proj(psA, td[2], lambda c: A1(3, c),
                       after=[tok_tkT[i], tok_w["Wk2"], tok_bk2f, tok_tqs[i]],
                       bias_row=ROWB["bk2f"])
            tok_tk2[i] = tk2
            tq2s = ve(lambda e: e.tensor_copy(tp[0][:, :], psC[:, :]),
                      after=[tq2, tok_tqT[i]], inc=True)
            tok_tq2s[i] = tq2s
            g_psC[0] = tq2s
            tmq = ve(lambda e: e.tensor_single_scalar(scr32[:, :], tp[0][:, :],
                                                      0.0, ALU.min),
                     after=[tq2s, tok_tek[i - 1] if i >= 1 else None], inc=True)
            teq = se(lambda e: e.activation(tp[4][:, :], scr32[:, :], AF.Exp),
                     after=[tmq, tok_ke[i - 1] if i >= 1 else None], inc=True)
            tqe = ve(lambda e: e.scalar_tensor_tensor(
                tp[2][:, :], tp[0][:, :], 0.0, tp[4][:, :], ALU.max, ALU.add),
                after=[teq, tok_S_pertile[i - 1] if i >= 1 else None], inc=True)
            tok_qe[i] = tqe
            tk2s = ve(lambda e: e.tensor_copy(tp[1][:, :], psA[:, :]),
                      after=[tk2, tok_tkT[i]], inc=True)
            tok_tk2s[i] = tk2s
            g_psA[0] = tk2s
            tmk = ve(lambda e: e.tensor_single_scalar(scr32[:, :], tp[1][:, :],
                                                      0.0, ALU.min),
                     after=[tk2s, teq], inc=True)
            tek = se(lambda e: e.activation(tp[4][:, :], scr32[:, :], AF.Exp),
                     after=[tmk, tqe], inc=True)
            tok_tek[i] = tek
            tke = ve(lambda e: e.scalar_tensor_tensor(
                tp[3][:, :], tp[1][:, :], 0.0, tp[4][:, :], ALU.max, ALU.add),
                after=[tek, tok_S_pertile[i - 1] if i >= 1 else None], inc=True)
            tok_ke[i] = tke
            tpq = ve(lambda e: e.tensor_mul(tp[5][:, :], tp[2][:, :],
                                            tp[3][:, :]),
                     after=[tke], inc=True)
            trd = ve(lambda e, q=qksl: e.reduce_sum(
                q.unsqueeze(-1), tp[5].ap().rearrange("p (h w) -> p h w", h=NH),
                axis=mybir.AxisListType.X), after=[tpq], inc=True)
            tad = ve(lambda e, q=qksl: e.tensor_scalar(q, q, EPS_QK, None,
                                                       ALU.add),
                     after=[trd], inc=True)
            tok_recip[i] = ve(lambda e, q=qksl,
                              o=recip[:, i * 16:(i + 1) * 16]:
                              e.reciprocal(o, q), after=[tad], inc=True)

        def p1_D(i):
            tS = None
            for h in range(NH):
                half = h % 2
                pr = h // 2
                dstps = (psS if half == 0 else psX)
                tS = te(lambda e, o=dstps[half * 64:half * 64 + 64,
                                          pr * 64:(pr + 1) * 64],
                        l=tp[3][:, h * 64:(h + 1) * 64],
                        r=tp[2][:, h * 64:(h + 1) * 64],
                        s=(i == 0 and h < 2), p=(i == nt - 1 and h >= NH - 2),
                        tpz=(0, half * 64):
                        e.matmul(o, l, r, start=s, stop=p, tile_position=tpz),
                        after=([tok_qe[i], tok_ke[i]]
                               + ([psx_chain[0]] if i == 0 else []))
                        if h <= 1 else (), inc=(h == NH - 1))
            tok_S_pertile[i] = tS

        for k in range(nt + 2):
            if 0 <= k - 1 < nt:
                p1_B(k - 1)
            if 0 <= k - 2 < nt:
                p1_D(k - 2)
            if k < nt:
                p1_A(k)
            if 0 <= k - 1 < nt:
                p1_C(k - 1)
        tok_S_last = tok_S_pertile[nt - 1]

        tok_Sev1 = ve(lambda e: e.tensor_copy(scr32[0:64, 0:512],
                                              psS[0:64, :]),
                      after=[tok_S_last], inc=True)
        tok_Sevac = ve(lambda e: e.tensor_copy(scr32[64:128, 0:512],
                                               psX[64:128, :]),
                       after=[tok_Sev1], inc=True)
        psx_chain[0] = tok_Sevac
        tok_Sdma = S.dma("qg", "qccS", dSin.ap(), scr32[:, 0:512],
                         after=[tok_Sevac])
        tok_cc1 = S.collective([dSin.ap().opt()], [dSout.ap().opt()], groups,
                               after=[tok_Sdma])
        tok_Sld = S.dma("qg", "qccS", scr32[:, 0:512], dSout.ap(),
                        after=[tok_cc1])
        tzs = ve(lambda e: e.memset(Sbf[:, :], 0.0),
                 after=[tok_S_pertile[nt - 1]], inc=True)
        sb3 = Sbf.ap().rearrange("p (q c) -> p q c", c=128)
        tpv = scr32.ap().rearrange("p (q c) -> p q c", c=64)
        tev = ve(lambda e: e.tensor_copy(sb3[0:64, :, 0:64],
                                         tpv[0:64, 0:8, :]),
                 after=[tzs, tok_Sld], inc=True)
        tok_Sbf = ve(lambda e: e.tensor_copy(sb3[64:128, :, 64:128],
                                             tpv[64:128, 0:8, :]),
                     after=[tev], inc=True)

        load_w("Wo", A1, 0, after=[tok_S_last])
        load_w("WrT", A1, 1, after=[tok_S_last])
        load_w("WkT", A1, 2, after=[tok_S_last])

        def bias_row_from(slot, dstb, after):
            tok = None
            for half in range(2):
                off = half * 512
                t = mm8(COL["sbc"], lambda c: A1(slot, c), off,
                        (list(after) + [psx_chain[0]]) if half == 0 else [tok])
                tok = ve(lambda e, o=off: e.tensor_copy(dstb[0:1, o:o + 512],
                                                        psX[0:1, 0:512]),
                         after=[t], inc=True)
            psx_chain[0] = tok
            return tok
        tok_sbr = tok_sbk = tok_gbt = tok_wvt = None

        def emit_ffn_setup():
            nonlocal tok_sbr, tok_sbk, tok_gbt, tok_wvt
            tok_sbr = bias_row_from(1, ROWB["sbr"], [tok_sbc, tok_w["WrT"]])
            tok_sbk = bias_row_from(2, ROWB["sbk"], [tok_w["WkT"]])
            tok_gbt = bcast_row2(dumpV, ROW["gb"], [tok_gbc, tok_hc[nt - 1]])
            for c in range(NC):
                tok_wvt = ve(lambda e, cc=c: e.tensor_mul(A2(1, cc), A2(1, cc),
                                                          dumpV[:, :]),
                             after=[tok_gbt, tok_w["WvT"]] if c == 0 else (),
                             inc=(c == NC - 1))

        fence = tok_S_last
        order = [nt - 1] + list(range(nt - 1))
        NP = nt
        tok_store = []
        tok_pc = [None, None]
        Z = lambda: [None] * (NP + 6)
        t_hcr, t_x2, t_v2l, t_v2T = (Z(), Z(), Z(), Z())
        t_out, t_os, t_oT, t_ao, t_y, t_x1a, t_x1, t_s1, t_h2c, t_h2cT = (
            Z(), Z(), Z(), Z(), Z(), Z(), Z(), Z(), Z(), Z())
        t_mixr, t_mixk, t_r, t_rs, t_kf, t_kr, t_ka, t_kaT, t_vf, t_t3 = (
            Z(), Z(), Z(), Z(), Z(), Z(), Z(), Z(), Z(), Z())
        t_rsx = Z()
        t_st = Z()

        murb = bc_last(COL["murc"], 128)
        mukb = bc_last(COL["mukc"], 128)

        def p2_P(p):
            j = order[p]
            if p < 2:
                xg = [tok_xread[nt - 2 + p] if nt >= 2 else None,
                      tok_sq[nt - 2 + p] if nt >= 2 else None]
                hg = [tok_spill[nt - 2 + p] if nt >= 2 else None,
                      tok_hcT[nt - 2 + p] if nt >= 2 else None]
            else:
                xg = [t_x1a[p - 2]]
                hg = [t_y[p - 2]]
            t_x2[p] = S.dma("qg", f"qx{p % 2}", xb[p % 2][:, :], xv[j],
                            after=xg)
            t_hcr[p] = S.dma("qs", "qhcr", hcb[p % 2][:, :], hcv[j],
                             after=[tok_spill[nt - 1]] + hg)

        def p2_pref(p):
            j = order[p]
            t_v2l[p] = S.dma("qs", "qv2", tp[0][:, :], v2v[j],
                             after=[t_v2T[p - 1] if p >= 1
                                    else tok_qe[nt - 1]])
            t_v2T[p] = transpose("qc", "qt2", td[2], tp[0],
                                 after=[t_v2l[p], t_out[p - 1] if p >= 1
                                        else tok_tk2[nt - 1]])

        def p2_out(p):
            j = order[p]
            to_ = None
            for pr in range(NH // 2):
                to_ = te(lambda e, o=psA[:, pr * 128:(pr + 1) * 128],
                         l=td[2][:, pr * 128:(pr + 1) * 128],
                         r=Sbf[:, pr * 128:(pr + 1) * 128],
                         s=(pr == 0 or pr == 4), pp=(pr == 3 or pr == 7):
                         e.matmul(o, l, r, start=s, stop=pp),
                         after=([t_v2T[p], tok_Sbf,
                                 t_os[p - 1] if p >= 1 else tok_tk2s[nt - 1]]
                                if pr == 0 else ()),
                         inc=(pr == NH // 2 - 1))
            t_out[p] = to_
            rexp = bc_last(recip[:, j * 16:(j + 1) * 16], 64)
            t_os[p] = ve(lambda e, rx=rexp: e.tensor_tensor(
                tp[2].ap().rearrange("p (h w) -> p h w", h=NH),
                psA.ap().rearrange("p (h w) -> p h w", h=NH), rx, ALU.mult),
                after=[to_, tok_recip[j],
                       t_oT[p - 1] if p >= 1 else fence], inc=True)
            t_oT[p] = transpose("qc", "qt3", td[3], tp[2],
                                after=[t_os[p],
                                       t_ao[p - 1] if p >= 1 else fence])

        def p2_ao(p):
            t_ao[p] = proj(psC, td[3], lambda c: A1(0, c),
                           after=[t_oT[p], tok_w["Wo"], tok_saborb,
                                  t_y[p - 1] if p >= 1
                                  else tok_tq2s[nt - 1]],
                           bias_row=ROWB["sabo"])

        def p2_R(p):
            j = order[p]
            st1 = stats[:, 0:16] if p % 2 == 0 else stats[:, 16:32]
            (ysum, ysq, ym, yw, ysd, yrstd, x1sum, x1sq, x1m, x1w, x1sd,
             x1rstd) = (st1[:, k:k + 1] for k in range(12))
            scg = t_s1[p - 2] if p >= 2 else None
            ty = ve(lambda e, hc_=hcb[p % 2]: e.scalar_tensor_tensor(
                tp[6][:, :], psC[:, :], 1.0, hc_[:, :], ALU.bypass, ALU.add),
                after=[t_ao[p], t_hcr[p],
                       t_t3[p - 2] if p >= 2 else t_v2st[nt - 1]], inc=True)
            t_y[p] = ty
            tva = ve(lambda e, a=ysum: e.tensor_scalar(
                dumpV[:, :], tp[6][:, :], 1.0, None, ALU.mult, op1=ALU.add,
                accum_out=a), after=[ty], inc=True)
            tsa = se(lambda e, a=ysq: e.activation(
                dumpS[:, :], tp[6][:, :], AF.Square, accum_out=a),
                after=[ty, scg], inc=True)
            ve(lambda e, a=ysum, m=ym: e.tensor_scalar(m, a, 1.0 / D, None,
                                                       ALU.mult), after=[tva])
            tw1 = ve(lambda e, a=ysum, m=ym, q=ysq, w=yw:
                     e.scalar_tensor_tensor(w, a, m, q, ALU.mult,
                                            ALU.subtract),
                     after=[tsa], inc=True)
            tln1 = se(lambda e, w=yw, s=ysd: e.activation(
                s, w, AF.Ln, bias=ccst[:, 1:2], scale=ccst[:, 2:3]),
                after=[tw1], inc=True)
            trx1 = se(lambda e, s=ysd, r=yrstd: e.activation(
                r, s, AF.Exp, scale=ccst[:, 3:4]), after=[tln1], inc=True)
            tt1 = ve(lambda e, m=ym, r=yrstd: e.tensor_scalar(
                tp[4][:, :], tp[6][:, :], m, r, ALU.subtract, ALU.mult),
                after=[trx1], inc=True)
            tu = ve(lambda e: e.tensor_mul(tp[5][:, :], tp[4][:, :],
                                           BCA["g2"][:, :]),
                    after=[tt1, tok_bca_g2], inc=True)
            tx1a = ve(lambda e, xb_=xb[p % 2]: e.tensor_add(
                dumpV[:, :], tp[5][:, :], xb_[:, :]),
                after=[tu, t_x2[p]], inc=True)
            t_x1a[p] = tx1a
            x1_buf = keepX if j == nt - 1 else x1_sl[p % 2]
            tx1 = ve(lambda e, o=x1_buf: e.tensor_add(o[:, :], dumpV[:, :],
                                                      BCA["b2"][:, :]),
                     after=[tx1a, tok_bca_b2,
                            t_t3[p - 2] if p >= 2 else None], inc=True)
            t_x1[p] = tx1
            tvb = ve(lambda e, a=x1sum, o=x1_buf: e.tensor_scalar(
                dumpV[:, :], o[:, :], 1.0, None, ALU.mult, op1=ALU.add,
                accum_out=a), after=[tx1], inc=True)
            tsb2 = se(lambda e, a=x1sq, o=x1_buf: e.activation(
                dumpS[:, :], o[:, :], AF.Square, accum_out=a),
                after=[tx1], inc=True)
            ve(lambda e, a=x1sum, m=x1m: e.tensor_scalar(m, a, 1.0 / D, None,
                                                         ALU.mult),
               after=[tvb])
            tw2 = ve(lambda e, a=x1sum, m=x1m, q=x1sq, w=x1w:
                     e.scalar_tensor_tensor(w, a, m, q, ALU.mult,
                                            ALU.subtract),
                     after=[tsb2], inc=True)
            tln2 = se(lambda e, w=x1w, s=x1sd: e.activation(
                s, w, AF.Ln, bias=ccst[:, 0:1], scale=ccst[:, 2:3]),
                after=[tw2], inc=True)
            trx2 = se(lambda e, s=x1sd, r=x1rstd: e.activation(
                r, s, AF.Exp, scale=ccst[:, 3:4]), after=[tln2], inc=True)
            ts1 = ve(lambda e, m=x1m, r=x1rstd, o=x1_buf: e.tensor_scalar(
                tp[4][:, :], o[:, :], m, r, ALU.subtract, ALU.mult),
                after=[trx2], inc=True)
            t_s1[p] = ts1
            th2c = ve(lambda e: e.tensor_mul(tp[5][:, :], tp[4][:, :],
                                             BCA["cb1"][:, :]),
                      after=[ts1, tok_bca_cb1], inc=True)
            h2cT_buf = keepT if j == nt - 1 else tdh
            th2cT = transpose("qs", "qtk" if j == nt - 1 else "qth",
                              h2cT_buf, tp[5],
                              after=[th2c] + ([modev[11]] if j == nt - 1
                                              else [t_mixk[p - 1]]))
            t_h2cT[p] = th2cT
            if j == nt - 1:
                tvh = ve(lambda e: e.tensor_tensor(
                    halos.ap().unsqueeze(-1), r8(keepT)[:, :, 127:128],
                    evenm.ap().unsqueeze(-1), ALU.mult),
                    after=[th2cT, tok_evenm], inc=True)
                thd = S.dma("qg", "qccH", dHin.ap(), halos[:, :], after=[tvh])
                tcc2 = S.collective([dHin.ap().opt()], [dHout.ap().opt()],
                                    groups, after=[thd])
                thr = S.dma("qg", "qccH", halos[:, :], dHout.ap(),
                            after=[tcc2])
                tb1 = ve(lambda e: e.tensor_mul(scr32[:, 0:8], halos[:, :],
                                                oddm[:, :]),
                         after=[thr, tok_oddm], inc=True)
                tb2 = ve(lambda e: e.tensor_mul(scr32[:, 8:16], COL["msbc"],
                                                evenm[:, :]),
                         after=[tb1, tok_msbc, tok_evenm], inc=True)
                tok_pc[0] = ve(lambda e: e.tensor_add(pcol[:, 0:8],
                                                      scr32[:, 0:8],
                                                      scr32[:, 8:16]),
                               after=[tb2], inc=True)
                t_mixk[p] = th2c
                return
            pcr = pcol[:, (j % 2) * 8:(j % 2) * 8 + 8]
            t1a = ve(lambda e: e.tensor_copy(
                r8(tp[1])[:, :, 1:128], r8(tdh)[:, :, 0:127]),
                after=[th2cT, tok_pc[j % 2],
                       t_kr[p - 2] if p >= 2 else None], inc=True)
            t1b = ve(lambda e, pc_=pcr: e.tensor_copy(
                r8(tp[1])[:, :, 0:1], pc_.unsqueeze(-1)), after=[t1a],
                inc=True)
            if j != nt - 1:
                pcw = pcol[:, ((j + 1) % 2) * 8:((j + 1) % 2) * 8 + 8]
                tok_pc[(j + 1) % 2] = ve(lambda e, pw=pcw: e.tensor_copy(
                    pw.unsqueeze(-1), r8(tdh)[:, :, 127:128]),
                    after=[th2cT], inc=True)
            td_ = ve(lambda e: e.tensor_sub(tp[5][:, :], tdh[:, :],
                                            tp[1][:, :]),
                     after=[t1b, th2cT], inc=True)
            tm1 = ve(lambda e: e.tensor_tensor(
                r8(dumpV), r8(tp[5]), murb, ALU.mult),
                after=[td_, tok_murc], inc=True)
            txr = ve(lambda e, o=xr_sl[p % 2]: e.tensor_add(
                o[:, :], dumpV[:, :], tp[1][:, :]),
                after=[tm1, t_r[p - 2] if p >= 2 else None], inc=True)
            t_mixr[p] = txr
            tm2 = ve(lambda e: e.tensor_tensor(
                r8(dumpV), r8(tp[5]), mukb, ALU.mult),
                after=[txr, tok_mukc], inc=True)
            txk = ve(lambda e, o=xk_sl[p % 2]: e.tensor_add(
                o[:, :], dumpV[:, :], tp[1][:, :]),
                after=[tm2, t_kf[p - 2] if p >= 2 else None], inc=True)
            t_mixk[p] = txk

        def p2_F_r(pf):
            t_r[pf] = proj(psB, xr_sl[pf % 2], lambda c: A1(1, c),
                           after=[t_mixr[pf], tok_w["WrT"], tok_sbr,
                                  t_kr[pf]],
                           bias_row=ROWB["sbr"])
            t_rsx[pf] = se(lambda e: e.activation(tp[6][:, :], psB[:, :],
                                                  AF.Exp, scale=ccst[:, 4:5]),
                           after=[t_r[pf], t_s1[pf + 1]], inc=True)
            ta1 = ve(lambda e: e.tensor_scalar(tp[6][:, :], tp[6][:, :], 1.0,
                                               None, ALU.add),
                     after=[t_rsx[pf]], inc=True)
            def _recip(e, nc=nc):
                with nc.allow_low_precision(reason="sigmoid tail, bf16 act"):
                    return e.reciprocal(tp[6][:, :], tp[6][:, :])
            t_rs[pf] = ve(_recip, after=[ta1], inc=True)

        def p2_F_kproj(pf, extra=()):
            t_kf[pf] = proj(psB, xk_sl[pf % 2], lambda c: A1(2, c),
                            after=[t_mixk[pf], tok_w["WkT"], tok_sbk,
                                   t_t3[pf - 1] if t_t3[pf - 1] is not None
                                   else tok_tks[nt - 1]] + list(extra),
                            bias_row=ROWB["sbk"])
            t_kr[pf] = ve(lambda e: e.tensor_single_scalar(
                tp[1][:, :], psB[:, :], 0.0, ALU.max),
                after=[t_kf[pf], t_mixk[pf + 1]], inc=True)
            t_ka[pf] = ve(lambda e: e.tensor_mul(tp[3][:, :], tp[1][:, :],
                                                 tp[1][:, :]),
                          after=[t_kr[pf], t_kaT[pf - 1]], inc=True)

        def p2_F_kaT(pf):
            t_kaT[pf] = transpose("qc", "qt2", td[0], tp[3],
                                  after=[t_ka[pf],
                                         t_vf[pf - 1] if t_vf[pf - 1]
                                         is not None else t_v2p[nt - 1]])

        def p2_F_v(pf, j, x1_buf):
            t_vf[pf] = proj(psB, td[0], lambda c: A2(1, c),
                            after=[t_kaT[pf], tok_wvt, t_rsx[pf]])
            t_t3[pf] = ve(lambda e: e.scalar_tensor_tensor(
                dumpV[:, :], psB[:, :], 1.0, tp[6][:, :], ALU.bypass,
                ALU.mult), after=[t_vf[pf], t_rs[pf]], inc=True)
            tx2 = ve(lambda e, o=x2b[pf % 2], xo=x1_buf: e.tensor_add(
                o[:, :], dumpV[:, :], xo[:, :]),
                after=[t_t3[pf], t_x1[pf], t_st[pf - 2] if pf >= 2 else None,
                       modev[11]],
                inc=True)
            t_st[pf] = S.dma("qg", "qst", yv[j], x2b[pf % 2][:, :],
                             after=[tx2])
            tok_store.append(t_st[pf])

        p2_P(0)
        p2_pref(0)
        for p in range(NP + 2):
            jq = order[p] if p < NP else None
            pf = p - 2
            jf = order[pf] if 0 <= pf < NP else None
            do_f = jf is not None and jf != nt - 1
            if jq is not None:
                p2_out(p)
            if p + 1 < NP:
                p2_pref(p + 1)
                p2_P(p + 1)
            if do_f:
                p2_F_kproj(pf)
                p2_F_r(pf)
            if jq is not None:
                p2_ao(p)
            if do_f:
                p2_F_kaT(pf)
                p2_F_v(pf, jf, x1_sl[pf % 2])
            if jq is not None:
                p2_R(p)
            if p == 1:
                emit_ffn_setup()

        jlast = nt - 1
        plast = NP + 2
        pcr = pcol[:, (jlast % 2) * 8:(jlast % 2) * 8 + 8]
        lastg = [t_t3[NP - 1], t_mixk[NP - 1], t_kr[NP - 1], t_rs[NP - 1]]
        t1a = ve(lambda e: e.tensor_copy(
            r8(tp[1])[:, :, 1:128], r8(keepT)[:, :, 0:127]),
            after=[t_h2cT[0], tok_pc[jlast % 2]] + lastg, inc=True)
        t1b = ve(lambda e: e.tensor_copy(
            r8(tp[1])[:, :, 0:1], pcr.unsqueeze(-1)), after=[t1a], inc=True)
        td_ = ve(lambda e: e.tensor_sub(tp[5][:, :], keepT[:, :], tp[1][:, :]),
                 after=[t1b], inc=True)
        tm1 = ve(lambda e: e.tensor_tensor(
            r8(dumpV), r8(tp[5]), murb, ALU.mult), after=[td_, tok_murc],
            inc=True)
        txr = ve(lambda e: e.tensor_add(xr_sl[0][:, :], dumpV[:, :],
                                        tp[1][:, :]),
                 after=[tm1, t_r[NP - 2]], inc=True)
        t_mixr[plast] = txr
        tm2 = ve(lambda e: e.tensor_tensor(
            r8(dumpV), r8(tp[5]), mukb, ALU.mult), after=[txr, tok_mukc],
            inc=True)
        txk = ve(lambda e: e.tensor_add(xk_sl[0][:, :], dumpV[:, :],
                                        tp[1][:, :]),
                 after=[tm2, t_kf[NP - 2]], inc=True)
        t_mixk[plast] = txk
        t_y[plast + 1] = txk
        t_s1[plast + 1] = txk
        t_mixk[plast + 1] = txk
        t_vf[plast - 1] = t_vf[NP - 1]
        p2_F_r(plast)
        p2_F_kproj(plast, extra=[t_rs[plast]])
        p2_F_kaT(plast)
        p2_F_v(plast, jlast, keepX)

        finals = {}
        for (s, v) in tok_store:
            finals[s] = max(finals.get(s, 0), v)
        for s, v in finals.items():
            S.ops["gpsimd"].append(Op(lambda e, ss=s, vv=v:
                                      e.wait_ge(sems[ss], vv), (), None, 1))

        with nc.Block() as block:
            S.emit(block)

    return nc


_NC_CACHE = {}

def _get_nc(ntiles=16):
    if ntiles not in _NC_CACHE:
        _NC_CACHE[ntiles] = build_nc(ntiles)
    return _NC_CACHE[ntiles]


def kernel(**inputs):
    L = int(inputs["L0_attn_x"].shape[1])
    toks = L // 2
    ntiles = toks // 128
    nc = _get_nc(ntiles)
    x = np.ascontiguousarray(np.asarray(inputs["L0_attn_x"], np.float32))
    row = lambda k: np.ascontiguousarray(np.asarray(inputs[k],
                                                    np.float32)).reshape(1, D)
    base = {
        "Wq": np.ascontiguousarray(np.asarray(inputs["L0_attn_Wq"], np.float32)),
        "Wk": np.ascontiguousarray(np.asarray(inputs["L0_attn_Wk"], np.float32)),
        "Wv": np.ascontiguousarray(np.asarray(inputs["L0_attn_Wv"], np.float32)),
        "Wq2": np.ascontiguousarray(np.asarray(inputs["L0_attn_Wq2"],
                                               np.float32)),
        "Wk2": np.ascontiguousarray(np.asarray(inputs["L0_attn_Wk2"],
                                               np.float32)),
        "Wv2": np.ascontiguousarray(np.asarray(inputs["L0_attn_Wv2"],
                                               np.float32)),
        "Wo": np.ascontiguousarray(np.asarray(inputs["L0_attn_Wo"], np.float32)),
        "WrT": np.ascontiguousarray(np.asarray(inputs["L0_ffn_W_r"],
                                               np.float32).T),
        "WkT": np.ascontiguousarray(np.asarray(inputs["L0_ffn_W_k"],
                                               np.float32).T),
        "WvT": np.ascontiguousarray(np.asarray(inputs["L0_ffn_W_v"],
                                               np.float32).T),
        "wmod": np.ascontiguousarray(np.asarray(inputs["L0_wmod"], np.float32)),
        "bq": row("L0_attn_bq"), "bk": row("L0_attn_bk"), "bv": row("L0_attn_bv"),
        "bq2": row("L0_attn_bq2"), "bk2": row("L0_attn_bk2"),
        "bv2": row("L0_attn_bv2"), "bo": row("L0_attn_bo"),
        "gamma": row("L0_attn_gamma"), "beta": row("L0_attn_beta"),
        "mu_r": row("L0_ffn_mu_r"), "mu_k": row("L0_ffn_mu_k"),
    }
    cmat = np.ascontiguousarray(np.asarray(inputs["L0_c"], np.float32))
    in_maps = []
    for core in range(8):
        b = core // 2
        half = core % 2
        m = dict(base)
        m["x"] = np.ascontiguousarray(x[b, half * toks:(half + 1) * toks, :])
        m["crow"] = cmat[b].reshape(1, D).copy()
        m["oddm"] = np.full((128, 8), float(half), np.float32)
        in_maps.append(m)
    res = run_bass_kernel_spmd(nc, in_maps, core_ids=list(range(8)))
    out = np.empty((4, L, D), np.float32)
    for core in range(8):
        b = core // 2
        half = core % 2
        out[b, half * toks:(half + 1) * toks, :] = res.results[core]["y"]
    return out


if __name__ == "__main__":
    ncb = build_nc(2)
    print("built", ncb.n_instructions(), "instructions")
